# version 08
# baseline (speedup 1.0000x reference)
import contextlib
import numpy as np

import concourse.bass as bass
import concourse.mybir as mybir
from concourse.bass_utils import run_bass_kernel_spmd

F32 = mybir.dt.float32
BF16 = mybir.dt.bfloat16
AF = mybir.ActivationFunctionType
ALU = mybir.AluOpType
AX = mybir.AxisListType

NCORE = 8
D = 2048
S = 8192
TC = S // NCORE
NT = TC // 128
HD = 128
E, IFF = 8, 1408
NI = IFF // 128
DC = D // 128
LN_EPS = 1e-5
RMS_EPS = 1e-6

def kernel_dma_keys_sync():
    return SY_DMA_KEYS


ENGINES = ("tensor", "vector", "scalar", "gpsimd", "sync")
GP_DMA_KEYS = tuple(f"gpd{i}" for i in range(4))
SY_DMA_KEYS = tuple(f"syd{i}" for i in range(2))
DMA_KEYS = GP_DMA_KEYS + SY_DMA_KEYS
SEMKEYS = ("tensor", "vector", "scalar", "gpsimd") + DMA_KEYS


class Tok:
    __slots__ = ("eng", "cnt")

    def __init__(self, eng, cnt):
        self.eng, self.cnt = eng, cnt


class Sched:

    def __init__(self, nc):
        self.nc = nc
        self.block = None
        self.sem = {}
        self.cnt = {e: 0 for e in SEMKEYS}
        self.hw = {e: {o: 0 for o in SEMKEYS} for e in ENGINES}
        self.pend = {e: [] for e in ENGINES}
        self.bar = {e: [] for e in ENGINES}
        self.awaited = {k: 0 for k in DMA_KEYS}
        self.qwait = {k: 0 for k in DMA_KEYS}
        self.rr = {"gpsimd": 0, "sync": 0}

    def init_sems(self, ctx):
        for e in SEMKEYS:
            self.sem[e] = ctx.enter_context(self.nc.semaphore(f"pg_{e}"))

    def barrier(self):
        toks = [Tok(e, self.cnt[e]) for e in SEMKEYS if self.cnt[e] > 0]
        for e in ENGINES:
            self.bar[e] = list(toks)

    def op(self, eng, fn, deps=(), dma=False, ndma=1):
        if dma:
            pool = GP_DMA_KEYS if eng == "gpsimd" else SY_DMA_KEYS
            key = pool[self.rr[eng] % len(pool)]
            self.rr[eng] += 1
        else:
            key = eng
        n = 16 * ndma if dma else 1
        alldeps = list(deps) + self.bar[eng]
        self.bar[eng] = []
        waits = []
        for d in alldeps:
            if d is None:
                continue
            if d.eng == key == "tensor":
                continue
            if d.eng in DMA_KEYS:
                want = self.cnt[d.eng]
                self.awaited[d.eng] = max(self.awaited[d.eng], want)
            else:
                want = d.cnt
            if want > self.hw[eng][d.eng]:
                self.hw[eng][d.eng] = want
                waits.append((d.eng, want))
        if dma and self.awaited.get(key, 0) > self.qwait[key]:
            self.qwait[key] = self.awaited[key]
            waits.append((key, self.qwait[key]))
        self.cnt[key] += n
        tok = Tok(key, self.cnt[key])
        self.pend[eng].append((waits, fn, n, dma, key))
        return tok

    def flush(self):
        for e in ENGINES:
            items = self.pend[e]
            if not items:
                continue
            self.pend[e] = []
            sems = self.sem

            def body(h, items=items, e=e):
                for waits, fn, n, dma, key in items:
                    for weng, wcnt in waits:
                        h.wait_ge(sems[weng], wcnt)
                    r = fn(h)
                    if dma:
                        instrs = r if isinstance(r, (list, tuple)) else [r]
                        assert len(instrs) * 16 == n, (n, len(instrs))
                        for ins in instrs:
                            ins.then_inc(sems[key], 16)
                    else:
                        r.then_inc(sems[key], n)
            getattr(self.block, e)(body)

    def last(self):
        return [Tok(e, self.cnt[e]) for e in SEMKEYS if self.cnt[e] > 0]


def build_program(upto="all"):
    nc = bass.Bass("TRN2", target_bir_lowering=False)

    x_in = nc.dram_tensor("x_in", [TC, D], F32, kind="ExternalInput").ap()
    y_out = nc.dram_tensor("y_out", [TC, D], F32, kind="ExternalOutput").ap()

    wq = nc.dram_tensor("wq", [4, D, 512], F32, kind="ExternalInput").ap()
    wk = nc.dram_tensor("wk", [D, 512], F32, kind="ExternalInput").ap()
    wv = nc.dram_tensor("wv", [D, 512], F32, kind="ExternalInput").ap()
    wo = nc.dram_tensor("wo", [4, D, 512], F32, kind="ExternalInput").ap()
    wgate = nc.dram_tensor("wgate", [D, E], F32, kind="ExternalInput").ap()
    wgb = nc.dram_tensor("wgb", [E, 3, D, 512], F32, kind="ExternalInput").ap()
    wub = nc.dram_tensor("wub", [E, 3, D, 512], F32, kind="ExternalInput").ap()
    wdb = nc.dram_tensor("wdb", [E, 4, IFF, 512], F32, kind="ExternalInput").ap()
    nw_attn = nc.dram_tensor("nw_attn", [128, DC], F32, kind="ExternalInput").ap()
    nw_ffn = nc.dram_tensor("nw_ffn", [128, DC], F32, kind="ExternalInput").ap()
    nw_q = nc.dram_tensor("nw_q", [1, HD], F32, kind="ExternalInput").ap()
    nw_k = nc.dram_tensor("nw_k", [1, HD], F32, kind="ExternalInput").ap()

    win = nc.dram_tensor("win", [12, D, 512], F32, kind="ExternalInput").ap()
    bin_ = nc.dram_tensor("bin", [12, 1, 512], F32, kind="ExternalInput").ap()
    wout = nc.dram_tensor("wout", [4, D, 512], F32, kind="ExternalInput").ap()
    bout = nc.dram_tensor("bout", [4, 1, 512], F32, kind="ExternalInput").ap()
    lnw = nc.dram_tensor("lnw", [128, DC], F32, kind="ExternalInput").ap()
    lnb = nc.dram_tensor("lnb", [128, DC], F32, kind="ExternalInput").ap()

    rope1c = nc.dram_tensor("rope1c", [TC, 64], F32, kind="ExternalInput").ap()
    rope1s = nc.dram_tensor("rope1s", [TC, 64], F32, kind="ExternalInput").ap()
    rope2c = nc.dram_tensor("rope2c", [TC, 512], F32, kind="ExternalInput").ap()
    rope2s = nc.dram_tensor("rope2s", [TC, 512], F32, kind="ExternalInput").ap()
    maska = nc.dram_tensor("maska", [4, 128, 512], F32, kind="ExternalInput").ap()
    masktA = nc.dram_tensor("masktA", [4, 128, 512], F32, kind="ExternalInput").ap()

    q_in = nc.dram_tensor("q_in", [2048, TC], BF16)
    q_out = nc.dram_tensor("q_out", [NCORE * 2048, TC], BF16)
    kv_in = nc.dram_tensor("kv_in", [1024, TC], BF16)
    kv_out = nc.dram_tensor("kv_out", [NCORE * 1024, TC], BF16)
    at_in = nc.dram_tensor("at_in", [2048, TC], BF16)
    at_out = nc.dram_tensor("at_out", [NCORE * 2048, TC], BF16)
    idxs = nc.dram_tensor("idxs", [128, 48], mybir.dt.uint32,
                          kind="ExternalInput").ap()
    a_in = nc.dram_tensor("a_in", [1, D], F32)
    a_out = nc.dram_tensor("a_out", [1, D], F32)
    RG = [list(range(NCORE))]

    with contextlib.ExitStack() as top:
        sb = lambda nm, shp, dt: top.enter_context(nc.sbuf_tensor(nm, shp, dt))
        x_sb = sb("x_sb", [128, NT * D], F32)
        ones_r = sb("ones_r", [1, 128], BF16)
        ones_rf = sb("ones_rf", [1, 128], F32)
        ones_cf = sb("ones_cf", [128, 1], F32)
        ones_cb = sb("ones_cb", [128, 1], BF16)
        eye_b = sb("eye_b", [128, 128], BF16)
        eye_f = sb("eye_f", [128, 128], F32)
        nwq_bc = sb("nwq_bc", [128, 128], F32)
        nwk_bc = sb("nwk_bc", [128, 128], F32)
        nwa_col = sb("nwa_col", [128, DC], F32)
        nwf_col = sb("nwf_col", [128, DC], F32)
        lnw_col = sb("lnw_col", [128, DC], F32)
        lnb_col = sb("lnb_col", [128, DC], F32)
        gw_sb = sb("gw_sb", [128, DC * E], BF16)
        r1c = sb("r1c", [128, NT * 64], F32)
        r1s = sb("r1s", [128, NT * 64], F32)
        stats = sb("stats", [128, 96], F32)
        cst = sb("cst", [128, 4], F32)
        rowb = sb("rowb", [1, 1024], F32)
        nmrow = sb("nmrow", [1, 1024], BF16)
        idx_sb = sb("idx_sb", [128, 48], mybir.dt.uint32)
        pb = [top.enter_context(nc.psum_tensor(f"pb{i}", [128, 512], F32)) for i in range(5)]
        ptr_ab = [top.enter_context(nc.psum_tensor(f"ptr{i}", [128, 512], BF16))
                  for i in range(2)]
        pmx = top.enter_context(nc.psum_tensor("pmx", [128, 512], F32))

        sch = Sched(nc)
        sch.init_sems(top)
        sch.block = top.enter_context(nc.Block())
        opn = sch.op

        res = {"x": [None] * NT, "pb": [None] * 5, "ptrg": [None, None],
               "pmx": None, "stats": None}
        tb_state = {"bank": 0}
        sca = lambda ab, c: stats[:, ab + c:ab + c + 1]

        def tpose_group(srcs, emit_evac, deps_src):
            bank = tb_state["bank"]
            tb_state["bank"] ^= 1
            pt = ptr_ab[bank]

            def tr(t, srcs=srcs, pt=pt):
                for i, s in enumerate(srcs):
                    m = t.transpose(pt[:, i * 128:(i + 1) * 128], s, eye_b[:])
                return m
            dtr = opn("tensor", tr, deps=list(deps_src) + [res["ptrg"][bank]])
            dev = opn("vector", lambda v, pt=pt: emit_evac(v, pt), deps=[dtr])
            res["ptrg"][bank] = dev
            return dev

        t_init0 = opn("gpsimd", lambda g: (
            g.memset(ones_r[:], 1.0), g.memset(ones_rf[:], 1.0),
            g.memset(ones_cf[:], 1.0), g.memset(ones_cb[:], 1.0),
            g.memset(cst[:, 0:1], LN_EPS), g.memset(cst[:, 1:2], RMS_EPS),
            g.memset(cst[:, 2:3], -0.5), g.memset(cst[:, 3:4], 1.0 / D),
            g.iota(eye_f[:], [[1, 128]], base=0, channel_multiplier=-1,
                   allow_small_or_imprecise_dtypes=True))[-1])
        t_init1 = opn("gpsimd", lambda g: g.tensor_scalar(
            out=eye_f[:], in0=eye_f[:], scalar1=0.0, scalar2=None,
            op0=ALU.is_equal), deps=[t_init0])
        t_init = opn("gpsimd", lambda g: g.tensor_copy(out=eye_b[:], in_=eye_f[:]),
                     deps=[t_init1])
        t_gw = opn("gpsimd", lambda g: g.dma_start(
            out=gw_sb[:].rearrange("p (c e) -> p c e", c=DC),
            in_=wgate.rearrange("(c p) e -> p c e", p=128)), dma=True)
        t_x = opn("sync", lambda s: s.dma_start(
            out=x_sb[:].rearrange("p (t d) -> p t d", t=NT),
            in_=x_in.rearrange("(t p) d -> p t d", p=128)), dma=True)
        t_cols = opn("sync", lambda s: [
            s.dma_start(out=nwa_col[:], in_=nw_attn),
            s.dma_start(out=nwf_col[:], in_=nw_ffn),
            s.dma_start(out=lnw_col[:], in_=lnw),
            s.dma_start(out=lnb_col[:], in_=lnb)], dma=True, ndma=4)
        t_idx = opn("sync", lambda s: s.dma_start(out=idx_sb[:], in_=idxs),
                    dma=True)
        t_rows = opn("sync", lambda s: [
            s.dma_start(out=rowb[0:1, 0:128], in_=nw_q),
            s.dma_start(out=rowb[0:1, 128:256], in_=nw_k)], dma=True, ndma=2)
        t_r1 = opn("sync", lambda s: [
            s.dma_start(out=r1c[:].rearrange("p (t k) -> p t k", t=NT),
                        in_=rope1c.rearrange("(t p) k -> p t k", p=128)),
            s.dma_start(out=r1s[:].rearrange("p (t k) -> p t k", t=NT),
                        in_=rope1s.rearrange("(t p) k -> p t k", p=128))], dma=True,
            ndma=2)
        t_b1 = opn("tensor", lambda t: t.matmul(pb[0][:, 0:128], ones_rf[:],
                                                rowb[0:1, 0:128], start=True, stop=True),
                   deps=[t_rows, t_cols, t_init])
        t_b2 = opn("tensor", lambda t: t.matmul(pb[1][:, 0:128], ones_rf[:],
                                                rowb[0:1, 128:256], start=True, stop=True),
                   deps=[t_rows])
        t_nbc = opn("vector", lambda v: (
            v.tensor_scalar(out=nwq_bc[:], in0=pb[0][:, 0:128],
                            scalar1=float(HD) ** -0.5, scalar2=None, op0=ALU.mult),
            v.tensor_copy(out=nwk_bc[:], in_=pb[1][:, 0:128]))[-1],
            deps=[t_b1, t_b2])
        res["pb"][0] = t_nbc
        res["pb"][1] = t_nbc
        res["rowb_r"] = t_b2

        def ffn_block(tag):
            sch.barrier()
            with contextlib.ExitStack() as sc_blk:
                sbb = lambda nm, shp, dt: sc_blk.enter_context(
                    nc.sbuf_tensor(f"{nm}_{tag}", shp, dt))
                xT = sbb("xT", [128, DC * TC], BF16)
                hid_stack = contextlib.ExitStack()
                sc_blk.callback(hid_stack.close)
                hid = hid_stack.enter_context(
                    nc.sbuf_tensor(f"hid_{tag}", [128, NT * D], BF16))
                t_hid = [None] * NT
                t_xT = [None] * (NT * DC)

                sc_a2 = contextlib.ExitStack()
                sba = lambda nm, shp, dt: sc_a2.enter_context(
                    nc.sbuf_tensor(f"{nm}_{tag}", shp, dt))
                wsl = sba("wsl", [128, 2 * DC * 512], BF16)
                ropeb = sba("ropeb", [128, 512], F32)
                qdmp = sba("qdmp", [128, 512], BF16)
                ropo = sba("ropo", [128, 1024], BF16)
                qst = sba("qst", [128, 1024], BF16)
                tmpa = sba("tmpa", [128, 256], F32)
                tmpb = sba("tmpb", [128, 256], F32)
                tmpc = sba("tmpc", [128, 256], F32)
                tmpd = sba("tmpd", [128, 256], F32)
                st = {"wsl_r": [None, None], "ropeb_w": None, "dump_r": None,
                      "tmp_r": None, "ropo_r": [None, None],
                      "qst_r": [[None] * 4 for _ in range(2)],
                      "stats": [None] * 4, "nbl": 0}

                def qkv_load(wsrc):
                    sl = st["nbl"] % 2
                    st["nbl"] += 1
                    wofs = sl * DC * 512
                    return opn("gpsimd", lambda g, wofs=wofs, wsrc=wsrc:
                               g.dma_start(
                        out=wsl[:, wofs:wofs + DC * 512].rearrange(
                            "p (c n) -> p c n", c=DC),
                        in_=wsrc.rearrange("(c p) n -> p c n", p=128)),
                        deps=[st["wsl_r"][sl]], dma=True)

                tw = [qkv_load(wk), qkv_load(wv)]

                with contextlib.ExitStack() as sc_ph:
                    h1s = sc_ph.enter_context(
                        nc.sbuf_tensor(f"h1s_{tag}", [128, 2 * D], BF16))
                    t_h1s_r = [None, None]
                    for j in range(NT):
                        xj = x_sb[:, j * D:(j + 1) * D]
                        ab = (j % 2) * 64
                        hb = (j % 2) * D
                        dsq = opn("scalar", lambda s, xj=xj, ab=ab, hb=hb:
                                  s.activation(
                                      out=h1s[:, hb:hb + D], in_=xj,
                                      func=AF.Square,
                                      accum_out=stats[:, ab:ab + 1]),
                                  deps=[t_x, res["x"][j], t_h1s_r[j % 2]])
                        dsm = opn("vector", lambda v, xj=xj, ab=ab: v.tensor_reduce(
                            out=sca(ab, 1), in_=xj, axis=AX.X, op=ALU.add, negate=True),
                            deps=[t_x, res["x"][j], t_h1s_r[j % 2]])
                        dv1 = opn("vector", lambda v, ab=ab: v.tensor_scalar(
                            out=sca(ab, 2), in0=sca(ab, 1),
                            scalar1=1.0 / D, scalar2=None, op0=ALU.mult), deps=[dsm])
                        dv2 = opn("vector", lambda v, ab=ab: v.tensor_tensor(
                            out=sca(ab, 6), in0=sca(ab, 2),
                            in1=sca(ab, 2), op=ALU.mult), deps=[dv1])
                        dvar = opn("vector", lambda v, ab=ab: v.scalar_tensor_tensor(
                            out=sca(ab, 3), in0=sca(ab, 0), scalar=1.0 / D,
                            in1=sca(ab, 6), op0=ALU.mult, op1=ALU.subtract),
                            deps=[dsq, dv2])
                        dr1a = opn("scalar", lambda s, ab=ab: s.activation(
                            out=sca(ab, 4), in_=sca(ab, 3), func=AF.Ln,
                            bias=cst[:, 0:1], scale=1.0), deps=[dvar, t_init0])
                        dr1 = opn("scalar", lambda s, ab=ab: s.activation(
                            out=sca(ab, 5), in_=sca(ab, 4), func=AF.Exp,
                            scale=cst[:, 2:3]), deps=[dr1a])
                        dms1 = opn("vector", lambda v, ab=ab: v.tensor_tensor(
                            out=sca(ab, 7), in0=sca(ab, 5),
                            in1=sca(ab, 5), op=ALU.mult), deps=[dr1])
                        dms = opn("vector", lambda v, ab=ab: v.tensor_tensor(
                            out=sca(ab, 8), in0=sca(ab, 3),
                            in1=sca(ab, 7), op=ALU.mult), deps=[dms1])
                        dr2a = opn("scalar", lambda s, ab=ab: s.activation(
                            out=sca(ab, 9), in_=sca(ab, 8), func=AF.Ln,
                            bias=cst[:, 1:2], scale=1.0), deps=[dms])
                        dr2 = opn("scalar", lambda s, ab=ab: s.activation(
                            out=sca(ab, 10), in_=sca(ab, 9),
                            func=AF.Exp, scale=cst[:, 2:3]), deps=[dr2a])
                        dh = opn("vector", lambda v, j=j, xj=xj, ab=ab:
                                 v.tensor_scalar(
                                     out=hid[:, j * D:(j + 1) * D], in0=xj,
                                     scalar1=sca(ab, 2), scalar2=sca(ab, 5),
                                     op0=ALU.add, op1=ALU.mult),
                                 deps=[dr1, t_hid[j]])
                        t_hid[j] = dh
                        dx1 = opn("vector", lambda v, ab=ab: (
                            v.tensor_tensor(out=sca(ab, 11), in0=sca(ab, 2),
                                            in1=sca(ab, 5), op=ALU.mult),
                            v.tensor_scalar(out=sca(ab, 12), in0=sca(ab, 5),
                                            scalar1=0.5, scalar2=1.0, op0=ALU.mult,
                                            op1=ALU.add))[-1], deps=[dr1])
                        dx2 = opn("vector", lambda v, ab=ab: v.tensor_scalar(
                            out=sca(ab, 13), in0=sca(ab, 11),
                            scalar1=0.5, scalar2=None, op0=ALU.mult), deps=[dx1])
                        dxu = opn("vector", lambda v, xj=xj, ab=ab: v.tensor_scalar(
                            out=xj, in0=xj, scalar1=sca(ab, 12),
                            scalar2=sca(ab, 13), op0=ALU.mult,
                            op1=ALU.add), deps=[dx2, dh, dsq])
                        res["x"][j] = dxu
                        dh1 = opn("scalar", lambda s, j=j, ab=ab, hb=hb:
                                  s.activation(
                                      out=h1s[:, hb:hb + D],
                                      in_=hid[:, j * D:(j + 1) * D], func=AF.Copy,
                                      scale=sca(ab, 10)), deps=[dr2, dh])
                        for g in range(4):
                            dcs = list(range(g * 4, g * 4 + 4))

                            def ev(v, pt, dcs=dcs, j=j):
                                for i, dc in enumerate(dcs):
                                    m = v.tensor_scalar(
                                        out=xT[:, dc * TC + j * 128:
                                               dc * TC + (j + 1) * 128],
                                        in0=pt[:, i * 128:(i + 1) * 128],
                                        scalar1=nwa_col[:, dc:dc + 1], scalar2=None,
                                        op0=ALU.mult)
                                return m
                            dev = tpose_group(
                                [h1s[:, hb + dc * 128:hb + (dc + 1) * 128]
                                 for dc in dcs],
                                ev, [dh1, res.get("xT_r")])
                            for dc in dcs:
                                t_xT[dc * NT + j] = dev
                            t_h1s_r[j % 2] = dev
                    res["xT_r"] = None
                    sch.flush()
                if upto == "a1":
                    sc_a2.close()
                    return

                t_store_q = []
                t_store_kv = []
                with sc_a2:

                    def qkv_nblock(kind, t_w, wofs, nbc, shard_rows):
                        last_mm = None
                        for j in range(NT):
                            bank = j % 4
                            s2 = j % 4
                            r2 = j % 2
                            rb0 = r2 * 512
                            sb_ = 16 + s2 * 12
                            qs = qst[:, r2 * 512:(r2 + 1) * 512]

                            def mm(t, j=j, bank=bank, wofs=wofs):
                                for dc in range(DC):
                                    m = t.matmul(pb[bank][:],
                                                 xT[:, dc * TC + j * 128: dc * TC + (j + 1) * 128],
                                                 wsl[:, wofs + dc * 512:
                                                     wofs + (dc + 1) * 512],
                                                 start=(dc == 0),
                                                 stop=(dc == DC - 1))
                                return m
                            dmm = opn("tensor", mm,
                                      deps=[t_w, res["pb"][bank]] +
                                           [t_xT[dc * NT + j] for dc in range(DC)])
                            last_mm = dmm
                            if kind == "v":
                                dev = opn("vector", lambda v, bank=bank, rb0=rb0:
                                          v.tensor_copy(
                                              out=ropo[:, rb0:rb0 + 512],
                                              in_=pb[bank][:]),
                                          deps=[dmm, st["ropo_r"][r2]])
                                res["pb"][bank] = dev

                                def ev(v, pt, qs=qs):
                                    for hh in range(4):
                                        m = v.tensor_copy(
                                            out=qs[:, hh * 128:(hh + 1) * 128],
                                            in_=pt[:, hh * 128:(hh + 1) * 128])
                                    return m
                                dev2 = tpose_group(
                                    [ropo[:, rb0 + hh * 128:rb0 + (hh + 1) * 128]
                                     for hh in range(4)],
                                    ev, [dev] + st["qst_r"][r2])
                                st["ropo_r"][r2] = dev2
                                toks = []
                                for hh in range(4):
                                    toks.append(opn("sync", lambda s, hh=hh, j=j,
                                                    qs=qs: s.dma_start(
                                        out=kv_in[512 + hh * 128: 512 + (hh + 1) * 128,
                                                  j * 128:(j + 1) * 128],
                                        in_=qs[:, hh * 128:(hh + 1) * 128]),
                                        deps=[dev2, res.get("kvin_w")], dma=True))
                                    st["qst_r"][r2][hh] = toks[-1]
                                t_store_kv.extend(toks)
                                continue
                            dsq = opn("scalar", lambda s, bank=bank,
                                      sb_=sb_: (
                                [s.activation(out=qdmp[:, hh * 128:(hh + 1) * 128],
                                              in_=pb[bank][:, hh * 128:(hh + 1) * 128],
                                              func=AF.Square,
                                              accum_out=stats[:, sb_ + hh:
                                                              sb_ + hh + 1])
                                 for hh in range(4)])[-1],
                                deps=[dmm, st["dump_r"], st["stats"][s2]])
                            st["dump_r"] = dsq
                            drsa = opn("scalar", lambda s, sb_=sb_: s.activation(
                                out=stats[:, sb_ + 4:sb_ + 8],
                                in_=stats[:, sb_:sb_ + 4],
                                func=AF.Ln, bias=cst[:, 1:2], scale=1.0 / HD),
                                deps=[dsq])
                            drs = opn("scalar", lambda s, sb_=sb_: s.activation(
                                out=stats[:, sb_ + 8:sb_ + 12],
                                in_=stats[:, sb_ + 4:sb_ + 8],
                                func=AF.Exp, scale=cst[:, 2:3]), deps=[drsa])
                            dnm = opn("vector", lambda v, bank=bank, nbc=nbc,
                                      sb_=sb_: (
                                [v.scalar_tensor_tensor(
                                    out=ropeb[:, hh * 128:(hh + 1) * 128],
                                    in0=pb[bank][:, hh * 128:(hh + 1) * 128],
                                    scalar=stats[:, sb_ + 8 + hh:sb_ + 9 + hh],
                                    in1=nbc,
                                    op0=ALU.mult, op1=ALU.mult) for hh in range(4)])[-1],
                                deps=[drs, t_nbc, st["ropeb_w"]])
                            st["stats"][s2] = dnm
                            res["pb"][bank] = dnm

                            def rope1f(v, j=j):
                                cs = r1c[:, j * 64:(j + 1) * 64]
                                sn = r1s[:, j * 64:(j + 1) * 64]
                                for hh in range(4):
                                    b1 = ropeb[:, hh * 128: hh * 128 + 64]
                                    b2 = ropeb[:, hh * 128 + 64: hh * 128 + 128]
                                    sl = slice(hh * 64, hh * 64 + 64)
                                    v.tensor_tensor(out=tmpa[:, sl], in0=b1, in1=cs,
                                                    op=ALU.mult)
                                    v.tensor_tensor(out=tmpb[:, sl], in0=b2, in1=sn,
                                                    op=ALU.mult)
                                    v.tensor_tensor(out=tmpc[:, sl], in0=b2, in1=cs,
                                                    op=ALU.mult)
                                    m = v.tensor_tensor(out=tmpd[:, sl], in0=b1, in1=sn,
                                                        op=ALU.mult)
                                return m
                            drpa = opn("vector", rope1f,
                                       deps=[dnm, t_r1, st["tmp_r"]])

                            def rope2f(v, rb0=rb0):
                                for hh in range(4):
                                    o1 = ropo[:, rb0 + hh * 128:
                                              rb0 + hh * 128 + 64]
                                    o2 = ropo[:, rb0 + hh * 128 + 64:
                                              rb0 + hh * 128 + 128]
                                    sl = slice(hh * 64, hh * 64 + 64)
                                    v.tensor_tensor(out=o1, in0=tmpa[:, sl],
                                                    in1=tmpb[:, sl], op=ALU.subtract)
                                    m = v.tensor_tensor(out=o2, in0=tmpc[:, sl],
                                                        in1=tmpd[:, sl], op=ALU.add)
                                return m
                            drp = opn("vector", rope2f,
                                      deps=[drpa, st["ropo_r"][r2]])
                            st["tmp_r"] = drp
                            st["ropeb_w"] = drp

                            def ev(v, pt, qs=qs):
                                for hh in range(4):
                                    m = v.tensor_copy(
                                        out=qs[:, hh * 128:(hh + 1) * 128],
                                        in_=pt[:, hh * 128:(hh + 1) * 128])
                                return m
                            dev = tpose_group(
                                [ropo[:, rb0 + hh * 128:rb0 + (hh + 1) * 128]
                                 for hh in range(4)],
                                ev, [drp] + st["qst_r"][r2])
                            st["ropo_r"][r2] = dev
                            dst_tgt = kv_in if kind == "k" else q_in
                            dst_list = t_store_kv if kind == "k" else t_store_q
                            dst_war = "kvin_w" if kind == "k" else "qin_w"
                            for hh in range(4):
                                toks = []
                                for row in shard_rows[hh]:
                                    toks.append(opn("sync", lambda s, hh=hh, row=row,
                                                    j=j, dst_tgt=dst_tgt, qs=qs:
                                                    s.dma_start(
                                        out=dst_tgt[row: row + 128,
                                                    j * 128:(j + 1) * 128],
                                        in_=qs[:, hh * 128:(hh + 1) * 128]),
                                        deps=[dev, res.get(dst_war)], dma=True))
                                st["qst_r"][r2][hh] = toks[-1]
                                dst_list.extend(toks)
                        return last_mm

                    st["wsl_r"][0] = qkv_nblock("k", tw[0], 0, nwk_bc[:],
                                                [[g * 128] for g in range(4)])
                    tw.append(qkv_load(wq[0]))
                    st["wsl_r"][1] = qkv_nblock("v", tw[1], DC * 512, None, None)
                    tw.append(qkv_load(wq[1]))
                    t_cc2 = opn("gpsimd", lambda g: g.collective_compute(
                        "AllGather", ALU.bypass, replica_groups=RG,
                        ins=[kv_in.ap().opt()], outs=[kv_out.ap().opt()]),
                        deps=t_store_kv + [res.get("kvout_r")])
                    res["kvin_w"] = t_cc2
                    for nb in range(4):
                        heads = [4 * nb + hh for hh in range(4)]
                        st["wsl_r"][nb % 2] = qkv_nblock(
                            "q", tw[2 + nb], (nb % 2) * DC * 512, nwq_bc[:],
                            [[h * 128] for h in heads])
                        if nb < 2:
                            tw.append(qkv_load(wq[nb + 2]))
                    sch.flush()

                t_cc1 = opn("gpsimd", lambda g: g.collective_compute(
                    "AllGather", ALU.bypass, replica_groups=RG,
                    ins=[q_in.ap().opt()], outs=[q_out.ap().opt()]),
                    deps=t_store_q + [res.get("qout_r")])
                res["qin_w"] = t_cc1
                if upto == "a2":
                    sch.flush()
                    return

                sch.barrier()
                at_toks = []
                with contextlib.ExitStack() as sc_ph:
                    sbp = lambda nm, shp, dt: sc_ph.enter_context(
                        nc.sbuf_tensor(f"{nm}_{tag}", shp, dt))
                    qTh = sbp("qTh", [128, S], BF16)
                    kTh = sbp("kTh", [128, S], BF16)
                    vstag = sbp("vstag", [128, 512], BF16)
                    v_sb = sbp("v_sb", [128, S], BF16)
                    pts = sbp("pts", [128, 2 * 512], BF16)
                    smsk = sbp("smsk", [128, 3 * 512], BF16)
                    sqs = sbp("sqs", [128, 64], F32)
                    esq = sbp("esq", [128, 64], F32)
                    msb = sbp("msb", [128, 4], BF16)
                    nmsbf = sbp("nmsbf", [128, 4], F32)
                    srec = sbp("srec", [128, 4], F32)
                    srecb = sbp("srecb", [128, 4], BF16)
                    tmpq = sbp("tmpq", [128, 52], F32)
                    rnb = sbp("rnb", [128, 512], F32)
                    atst = sbp("atst", [128, 512], BF16)
                    mka = sbp("mka", [128, 4 * 512], BF16)
                    mkt = sbp("mkt", [128, 4 * 512], BF16)
                    t_mk = opn("gpsimd", lambda g: (
                        [g.dma_start(out=mka[:, tt * 512:(tt + 1) * 512],
                                     in_=maska[tt]) for tt in range(4)] +
                        [g.dma_start(out=mkt[:, tt * 512:(tt + 1) * 512],
                                     in_=masktA[tt]) for tt in range(4)]),
                        dma=True, ndma=8)
                    ast = {"qkh_r": None, "atst_r": None, "smsk_r": None,
                           "nmrow_r": [None, None], "t_pts": [None, None],
                           "t_smsk2": [None, None], "sqs_r": None,
                           "esq_r": None, "msb_r": None, "t_qkh": None,
                           "sq_r": None, "srecb_r": None, "srow_r0": None,
                           "srow_r1": None, "last_av": None, "dump_r": None}
                    sbanks = (pb[2], pb[3], pmx)

                    def sb_get(i):
                        return res["pmx"] if i == 2 else res["pb"][2 + i]

                    def sb_set(i, tok):
                        if i == 2:
                            res["pmx"] = tok
                        else:
                            res["pb"][2 + i] = tok

                    def emit_pass1_gen(qc):
                        p1_smsk_r = ast["smsk_r"]
                        for tt in range(4):
                            for kc in range(qc + 1):
                                n = tt * (qc + 1) + kc
                                bank = n % 2
                                qlo = qc * 512 + tt * 128
                                col = slice(tt * 16 + kc, tt * 16 + kc + 1)

                                def mm(t, qlo=qlo, kc=kc, bank=bank):
                                    return t.matmul(pb[bank][:],
                                                    qTh[:, qlo:qlo + 128],
                                                    kTh[:, kc * 512:(kc + 1) * 512],
                                                    start=True, stop=True)
                                dmm = opn("tensor", mm,
                                          deps=[ast["t_qkh"], res["pb"][bank]])
                                if kc == qc:
                                    dms = opn("vector", lambda v, bank=bank, tt=tt:
                                              v.scalar_tensor_tensor(
                                                  out=smsk[:, 1024:1536],
                                                  in0=pb[bank][:], scalar=1.0,
                                                  in1=mka[:, tt * 512:(tt + 1) * 512],
                                                  op0=ALU.mult, op1=ALU.add),
                                              deps=[dmm, t_mk, p1_smsk_r])
                                    drd = opn("vector", lambda v, col=col:
                                              v.tensor_reduce(
                                                  out=sqs[:, col],
                                                  in_=smsk[:, 1024:1536], axis=AX.X,
                                                  op=ALU.max, negate=True),
                                              deps=[dms, ast["sqs_r"]])
                                    dsx = opn("scalar", lambda s, col=col:
                                              s.activation(
                                                  out=vstag[:],
                                                  in_=smsk[:, 1024:1536],
                                                  func=AF.Exp, bias=sqs[:, col],
                                                  scale=1.0,
                                                  accum_out=esq[:, col]),
                                              deps=[drd, ast["esq_r"],
                                                    ast["dump_r"], t_vsb])
                                    p1_smsk_r = dsx
                                    res["pb"][bank] = dms
                                else:
                                    drd = opn("vector", lambda v, col=col,
                                              bank=bank: v.tensor_reduce(
                                                  out=sqs[:, col],
                                                  in_=pb[bank][:], axis=AX.X,
                                                  op=ALU.max, negate=True),
                                              deps=[dmm, ast["sqs_r"]])
                                    dsx = opn("scalar", lambda s, col=col,
                                              bank=bank: s.activation(
                                                  out=vstag[:],
                                                  in_=pb[bank][:],
                                                  func=AF.Exp, bias=sqs[:, col],
                                                  scale=1.0,
                                                  accum_out=esq[:, col]),
                                              deps=[drd, ast["esq_r"],
                                                    ast["dump_r"], t_vsb])
                                    res["pb"][bank] = dsx
                                ast["dump_r"] = dsx
                                yield
                        ast["smsk_r"] = p1_smsk_r

                    def emit_mrow(qc):
                        nc_ = qc + 1
                        ns = qc % 2
                        dmn = opn("vector", lambda v, nc_=nc_: (
                            [v.tensor_reduce(out=msb[:, tt:tt + 1],
                                             in_=sqs[:, tt * 16: tt * 16 + nc_],
                                             axis=AX.X, op=ALU.min)
                             for tt in range(4)])[-1],
                            deps=[ast["msb_r"],
                                  Tok("vector", sch.cnt["vector"]),
                                  Tok("scalar", sch.cnt["scalar"])])
                        dmf = opn("vector", lambda v: v.tensor_copy(
                            out=nmsbf[:], in_=msb[:]), deps=[dmn])
                        dS = None
                        prev_dte = None
                        for tt in range(4):
                            dts = opn("vector", lambda v, tt=tt, nc_=nc_:
                                      v.tensor_scalar(
                                          out=tmpq[:, 0:nc_],
                                          in0=sqs[:, tt * 16: tt * 16 + nc_],
                                          scalar1=nmsbf[:, tt:tt + 1],
                                          scalar2=-1.0, op0=ALU.subtract,
                                          op1=ALU.mult),
                                      deps=[dmf, prev_dte])
                            dte = opn("scalar", lambda s, nc_=nc_: s.activation(
                                out=tmpq[:, 16:16 + nc_], in_=tmpq[:, 0:nc_],
                                func=AF.Exp), deps=[dts])
                            prev_dte = dte
                            dtm = opn("vector", lambda v, tt=tt, nc_=nc_:
                                      v.tensor_tensor(
                                          out=tmpq[:, 32:32 + nc_],
                                          in0=tmpq[:, 16:16 + nc_],
                                          in1=esq[:, tt * 16: tt * 16 + nc_],
                                          op=ALU.mult), deps=[dte])
                            dsm = opn("vector", lambda v, nc_=nc_: v.tensor_reduce(
                                out=tmpq[:, 48:49], in_=tmpq[:, 32:32 + nc_],
                                axis=AX.X, op=ALU.add), deps=[dtm])
                            dS = opn("vector", lambda v, tt=tt: v.reciprocal(
                                out=srec[:, tt:tt + 1], in_=tmpq[:, 48:49]),
                                deps=[dsm, ast["sq_r"]])
                        ast["sqs_r"] = dS
                        ast["esq_r"] = dS
                        bank = tb_state["bank"]
                        tb_state["bank"] ^= 1
                        pt = ptr_ab[bank]
                        dtr = opn("tensor", lambda t, pt=pt: (
                            [t.transpose(pt[0:1, tt * 128:(tt + 1) * 128],
                                         msb[:, tt:tt + 1], eye_b[:])
                             for tt in range(4)])[-1],
                            deps=[dmn, res["ptrg"][bank]])
                        ast["msb_r"] = dtr
                        dnm = opn("vector", lambda v, pt=pt, ns=ns: v.tensor_copy(
                            out=nmrow[0:1, ns * 512:(ns + 1) * 512],
                            in_=pt[0:1, 0:512]),
                            deps=[dtr, ast["nmrow_r"][ns]])
                        res["ptrg"][bank] = dnm
                        dcp = opn("vector", lambda v: v.tensor_copy(
                            out=srecb[:], in_=srec[:]),
                            deps=[dS, ast["srecb_r"]])
                        ast["sq_r"] = dcp
                        bank2 = tb_state["bank"]
                        tb_state["bank"] ^= 1
                        pt2 = ptr_ab[bank2]
                        dtr2 = opn("tensor", lambda t, pt2=pt2: (
                            [t.transpose(pt2[0:1, tt * 128:(tt + 1) * 128],
                                         srecb[:, tt:tt + 1], eye_b[:])
                             for tt in range(4)])[-1],
                            deps=[dcp, res["ptrg"][bank2]])
                        ast["srecb_r"] = dtr2
                        dsr = opn("vector", lambda v, pt2=pt2, ns=ns: v.tensor_copy(
                            out=rowb[0:1, ns * 512:(ns + 1) * 512],
                            in_=pt2[0:1, 0:512]),
                            deps=[dtr2, ast[f"srow_r{ns}"], res.get("rowb_r")])
                        res["ptrg"][bank2] = dsr
                        res["rowb_r"] = dsr
                        return dnm, dsr

                    def emit_pass2_gen(qc, dnm, dsr, hl):
                        nkc = (qc + 1) * 4
                        ns = qc % 2
                        nmr = nmrow[0:1, ns * 512:(ns + 1) * 512]
                        pend = None

                        def em_av(kc, dex, nkc=nkc):
                            def av(t, kc=kc, nkc=nkc):
                                return t.matmul(
                                    pb[4][:], v_sb[:, kc * 128:(kc + 1) * 128],
                                    pts[:, (kc % 2) * 512:(kc % 2 + 1) * 512],
                                    start=(kc == 0), stop=(kc == nkc - 1))
                            davk = opn("tensor", av,
                                       deps=[dex, t_vsb] +
                                            ([res["pb"][4]] if kc == 0 else []))
                            ast["t_pts"][kc % 2] = davk
                            return davk

                        for kc in range(nkc):
                            bi = kc % 3
                            bank = sbanks[bi]

                            def mm(t, kc=kc, bank=bank, qc=qc, nmr=nmr):
                                t.matmul(bank[:],
                                         kTh[:, kc * 128:(kc + 1) * 128],
                                         qTh[:, qc * 512:(qc + 1) * 512],
                                         start=True, stop=False)
                                return t.matmul(bank[:], ones_r[:], nmr,
                                                start=False, stop=True)
                            dmm = opn("tensor", mm, deps=[dnm, sb_get(bi)])
                            ast["nmrow_r"][ns] = dmm
                            if kc >= qc * 4:
                                kl = kc - qc * 4
                                dmk = opn("vector", lambda v, bank=bank, kl=kl,
                                          kc=kc: v.scalar_tensor_tensor(
                                              out=smsk[:, (kc % 2) * 512:
                                                      (kc % 2 + 1) * 512],
                                              in0=bank[:], scalar=1.0,
                                              in1=mkt[:, kl * 512:(kl + 1) * 512],
                                              op0=ALU.mult, op1=ALU.add),
                                          deps=[dmm, t_mk, ast["t_smsk2"][kc % 2]])
                                dex = opn("scalar", lambda s, kc=kc: s.activation(
                                    out=pts[:, (kc % 2) * 512:(kc % 2 + 1) * 512],
                                    in_=smsk[:, (kc % 2) * 512:(kc % 2 + 1) * 512],
                                    func=AF.Exp), deps=[dmk, ast["t_pts"][kc % 2]])
                                ast["t_smsk2"][kc % 2] = dex
                                sb_set(bi, dmk)
                            else:
                                dex = opn("scalar", lambda s, bank=bank, kc=kc:
                                          s.activation(
                                              out=pts[:, (kc % 2) * 512:
                                                      (kc % 2 + 1) * 512],
                                              in_=bank[:], func=AF.Exp),
                                          deps=[dmm, ast["t_pts"][kc % 2]])
                                sb_set(bi, dex)
                            if pend is not None:
                                em_av(*pend)
                            pend = (kc, dex)
                            yield
                        last_av = em_av(*pend)
                        dbc = opn("tensor", lambda t, ns=ns: t.matmul(
                            pb[3][:], ones_rf[:],
                            rowb[0:1, ns * 512:(ns + 1) * 512],
                            start=True, stop=True), deps=[dsr, res["pb"][3]])
                        ast[f"srow_r{ns}"] = dbc
                        res["rowb_r"] = dbc
                        dnorm0 = opn("vector", lambda v: v.tensor_copy(
                            out=rnb[:], in_=pb[3][:]), deps=[dbc])
                        dnorm = opn("vector", lambda v: v.tensor_tensor(
                            out=atst[:], in0=pb[4][:], in1=rnb[:], op=ALU.mult),
                            deps=[dnorm0, last_av, ast["atst_r"]])
                        res["pb"][3] = dnorm
                        res["pb"][4] = dnorm
                        dst = opn("sync", lambda s, qc=qc, hl=hl: s.dma_start(
                            out=at_in[(qc // 2) * 256 + hl * 128:
                                      (qc // 2) * 256 + (hl + 1) * 128,
                                      (qc % 2) * 512:(qc % 2) * 512 + 512],
                            in_=atst[:]), deps=[dnorm, res.get("atin_w")], dma=True)
                        ast["atst_r"] = dst
                        at_toks.append(dst)
                        ast["last_av"] = last_av
                        yield

                    def lk(g):
                        return [g.indirect_dma_start(
                            out=kTh[:, i * TC:(i + 1) * TC], out_offset=None,
                            in_=kv_out.ap(),
                            in_offset=bass.IndirectOffsetOnAxis(
                                ap=idx_sb[:, 16 + i:17 + i], axis=0))
                            for i in range(NCORE)]
                    t_kh = opn("gpsimd", lk, deps=[t_cc1, t_idx], dma=True, ndma=8)
                    t_vsb = None
                    for i in range(NCORE):
                        for g2 in range(2):
                            t_vg = opn("gpsimd", lambda g, i=i, g2=g2:
                                       g.indirect_dma_start(
                                out=vstag[:], out_offset=None, in_=kv_out.ap(),
                                in_offset=bass.IndirectOffsetOnAxis(
                                    ap=idx_sb[:, 24 + i:25 + i], axis=0),
                                element_offset=g2 * 512),
                                deps=[t_cc2, t_idx, t_vsb], dma=True)
                            bs = [g2 * 4 + bi for bi in range(4)]

                            def ev(v, pt, i=i, bs=bs):
                                for ii, b in enumerate(bs):
                                    m = v.tensor_copy(
                                        out=v_sb[:, (i * 8 + b) * 128:
                                                 (i * 8 + b + 1) * 128],
                                        in_=pt[:, ii * 128:(ii + 1) * 128])
                                return m
                            t_vsb = tpose_group(
                                [vstag[:, bi * 128:(bi + 1) * 128]
                                 for bi in range(4)],
                                ev, [t_vg])

                    for hl in range(2):
                        def lq(g, hl=hl):
                            return [g.indirect_dma_start(
                                out=qTh[:, i * TC:(i + 1) * TC], out_offset=None,
                                in_=q_out.ap(),
                                in_offset=bass.IndirectOffsetOnAxis(
                                    ap=idx_sb[:, hl * 8 + i:hl * 8 + i + 1], axis=0))
                                for i in range(NCORE)]
                        ast["t_qkh"] = opn("gpsimd", lq,
                                           deps=[t_cc1, t_idx, ast["qkh_r"], t_kh],
                                           dma=True, ndma=8)
                        prev = None
                        for qc in range(16):
                            g1 = emit_pass1_gen(qc)
                            g2 = (emit_pass2_gen(prev[0], prev[1], prev[2], hl)
                                  if prev is not None else None)
                            a1, a2 = True, g2 is not None
                            while a1 or a2:
                                if a1:
                                    try:
                                        next(g1)
                                    except StopIteration:
                                        a1 = False
                                if a2:
                                    try:
                                        next(g2)
                                    except StopIteration:
                                        a2 = False
                            dnm_q, dsr_q = emit_mrow(qc)
                            prev = (qc, dnm_q, dsr_q)
                        for _ in emit_pass2_gen(prev[0], prev[1], prev[2], hl):
                            pass

                        ast["qkh_r"] = ast["last_av"]
                    res["qout_r"] = ast["qkh_r"]
                    res["kvout_r"] = ast["qkh_r"]
                    sch.flush()
                if upto == "attn":
                    return

                t_cc3 = opn("gpsimd", lambda g: g.collective_compute(
                    "AllGather", ALU.bypass, replica_groups=RG,
                    ins=[at_in.ap().opt()], outs=[at_out.ap().opt()]),
                    deps=at_toks + [res.get("atout_r")])
                res["atin_w"] = t_cc3
                sch.barrier()
                with contextlib.ExitStack() as sc_ph:
                    sbp = lambda nm, shp, dt: sc_ph.enter_context(
                        nc.sbuf_tensor(f"{nm}_{tag}", shp, dt))
                    attg = sbp("attg", [128, DC * TC], BF16)
                    wso = sbp("wso", [128, DC * 512], BF16)

                    def la(g):
                        return [g.indirect_dma_start(
                            out=attg[:, ic * TC:(ic + 1) * TC], out_offset=None,
                            in_=at_out.ap(),
                            in_offset=bass.IndirectOffsetOnAxis(
                                ap=idx_sb[:, 32 + ic:33 + ic], axis=0))
                            for ic in range(DC)]
                    t_ag = opn("gpsimd", la, deps=[t_cc3, t_idx], dma=True, ndma=16)
                    res["atout_r"] = t_ag
                    t_wso_r = None
                    for nb in range(4):
                        t_wo = opn("gpsimd", lambda g, nb=nb: g.dma_start(
                            out=wso[:].rearrange("p (c n) -> p c n", c=DC),
                            in_=wo[nb].rearrange("(c p) n -> p c n", p=128)),
                            deps=[t_wso_r], dma=True)
                        for j in range(NT):
                            bank = j % 4

                            def mm(t, j=j, bank=bank):
                                for ic in range(DC):
                                    m = t.matmul(pb[bank][:],
                                                 attg[:, ic * TC + j * 128:
                                                      ic * TC + (j + 1) * 128],
                                                 wso[:, ic * 512:(ic + 1) * 512],
                                                 start=(ic == 0), stop=(ic == DC - 1))
                                return m
                            dmm = opn("tensor", mm, deps=[t_wo, t_ag, res["pb"][bank]])
                            t_wso_r = dmm
                            dup = opn("vector", lambda v, j=j, nb=nb, bank=bank: (
                                v.tensor_tensor(
                                    out=hid[:, j * D + nb * 512: j * D + (nb + 1) * 512],
                                    in0=hid[:, j * D + nb * 512: j * D + (nb + 1) * 512],
                                    in1=pb[bank][:], op=ALU.add),
                                v.scalar_tensor_tensor(
                                    out=x_sb[:, j * D + nb * 512: j * D + (nb + 1) * 512],
                                    in0=pb[bank][:], scalar=0.5,
                                    in1=x_sb[:, j * D + nb * 512: j * D + (nb + 1) * 512],
                                    op0=ALU.mult, op1=ALU.add))[-1],
                                deps=[dmm, t_hid[j], res["x"][j]])
                            t_hid[j] = dup
                            res["x"][j] = dup
                            res["pb"][bank] = dup
                    sch.flush()
                if upto == "oproj":
                    return

                hid_stack.close_after_experts = True
                sch.barrier()
                with contextlib.ExitStack() as sc_ph:
                    h2s = sc_ph.enter_context(
                        nc.sbuf_tensor(f"h2s_{tag}", [128, D], BF16))
                    t_h2s_r = None
                    for j in range(NT):
                        dsq = opn("scalar", lambda s, j=j: s.activation(
                            out=h2s[:], in_=hid[:, j * D:(j + 1) * D], func=AF.Square,
                            accum_out=stats[:, 0:1]),
                            deps=[t_hid[j], t_h2s_r, res["stats"]])
                        drsa = opn("scalar", lambda s: s.activation(
                            out=stats[:, 1:2], in_=stats[:, 0:1], func=AF.Ln,
                            bias=cst[:, 1:2], scale=1.0 / D), deps=[dsq])
                        drs = opn("scalar", lambda s: s.activation(
                            out=stats[:, 2:3], in_=stats[:, 1:2],
                            func=AF.Exp, scale=cst[:, 2:3]), deps=[drsa])
                        res["stats"] = drs
                        dh2 = opn("scalar", lambda s, j=j: s.activation(
                            out=h2s[:], in_=hid[:, j * D:(j + 1) * D], func=AF.Copy,
                            scale=stats[:, 2:3]), deps=[drs])
                        for g in range(4):
                            dcs = list(range(g * 4, g * 4 + 4))

                            def ev(v, pt, dcs=dcs, j=j):
                                for i, dc in enumerate(dcs):
                                    m = v.tensor_scalar(
                                        out=xT[:, dc * TC + j * 128:
                                               dc * TC + (j + 1) * 128],
                                        in0=pt[:, i * 128:(i + 1) * 128],
                                        scalar1=nwf_col[:, dc:dc + 1], scalar2=None,
                                        op0=ALU.mult)
                                return m
                            dev = tpose_group(
                                [h2s[:, dc * 128:(dc + 1) * 128] for dc in dcs],
                                ev, [dh2] + [t_xT[dc * NT + j] for dc in dcs])
                            for dc in dcs:
                                t_xT[dc * NT + j] = dev
                            t_h2s_r = dev
                    sch.flush()

                hid_stack.close()
                sch.barrier()
                with contextlib.ExitStack() as sc_ph:
                    sbp = lambda nm, shp, dt: sc_ph.enter_context(
                        nc.sbuf_tensor(f"{nm}_{tag}", shp, dt))
                    xeT = sbp("xeT", [128, NI * TC], BF16)
                    wga = sbp("wga", [128, DC * 512], BF16)
                    wgb_s = sbp("wgb2", [128, DC * 512], BF16)
                    wda = sbp("wda", [128, NI * 512], BF16)
                    wdb_s = sbp("wdb2", [128, NI * 512], BF16)
                    comb = sbp("comb", [128, NT * E], F32)
                    est = {"wsl": [None, None], "dsl": [None, None],
                           "xeT_w": [[None] * 2 for _ in range(NI)],
                           "xeT_r": None, "xeT_last": None, "comb": [None] * NT,
                           "xT_last": None}
                    for j in range(NT):
                        bank = j % 4

                        def mm(t, j=j, bank=bank):
                            for dc in range(DC):
                                m = t.matmul(pb[bank][:, 0:E],
                                             xT[:, dc * TC + j * 128: dc * TC + (j + 1) * 128],
                                             gw_sb[:, dc * E:(dc + 1) * E],
                                             start=(dc == 0), stop=(dc == DC - 1))
                            return m
                        dmm = opn("tensor", mm,
                                  deps=[t_gw, res["pb"][bank]] +
                                       [t_xT[dc * NT + j] for dc in range(DC)])
                        dcp0 = opn("vector", lambda v, bank=bank: v.tensor_copy(
                            out=stats[:, 40:48], in_=pb[bank][:, 0:E]),
                            deps=[dmm, res["stats"]])
                        dcp = opn("vector", lambda v: v.tensor_reduce(
                            out=stats[:, 32:33], in_=stats[:, 40:48],
                            axis=AX.X, op=ALU.max, negate=True), deps=[dcp0])
                        res["pb"][bank] = dcp0
                        cj = comb[:, j * E:(j + 1) * E]
                        dex = opn("scalar", lambda s, cj=cj: s.activation(
                            out=cj, in_=stats[:, 40:48],
                            func=AF.Exp, bias=stats[:, 32:33], scale=1.0), deps=[dcp])
                        g1 = opn("vector", lambda v, cj=cj: v.tensor_reduce(
                            out=stats[:, 33:34], in_=cj, axis=AX.X, op=ALU.add),
                            deps=[dex])
                        g2 = opn("vector", lambda v: v.reciprocal(
                            out=stats[:, 34:35], in_=stats[:, 33:34]), deps=[g1])
                        g3 = opn("vector", lambda v, cj=cj: v.tensor_scalar(
                            out=cj, in0=cj, scalar1=stats[:, 34:35],
                            scalar2=None, op0=ALU.mult), deps=[g2])
                        g4 = opn("vector", lambda v, cj=cj: v.tensor_reduce(
                            out=stats[:, 35:36], in_=cj, axis=AX.X, op=ALU.max),
                            deps=[g3])
                        g5 = opn("vector", lambda v, cj=cj: v.tensor_scalar(
                            out=stats[:, 40:48], in0=cj, scalar1=stats[:, 35:36],
                            scalar2=None, op0=ALU.is_ge), deps=[g4])
                        g6 = opn("vector", lambda v, cj=cj: v.tensor_tensor(
                            out=stats[:, 40:48], in0=cj, in1=stats[:, 40:48],
                            op=ALU.subtract), deps=[g5])
                        g7 = opn("vector", lambda v: v.tensor_reduce(
                            out=stats[:, 36:37], in_=stats[:, 40:48],
                            axis=AX.X, op=ALU.max), deps=[g6])
                        g8 = opn("vector", lambda v: (
                            v.tensor_tensor(out=stats[:, 37:38], in0=stats[:, 35:36],
                                            in1=stats[:, 36:37], op=ALU.add)),
                            deps=[g7])
                        g9 = opn("vector", lambda v: v.tensor_scalar(
                            out=stats[:, 37:38], in0=stats[:, 37:38],
                            scalar1=1e-9, scalar2=None, op0=ALU.add), deps=[g8])
                        g10 = opn("vector", lambda v: v.reciprocal(
                            out=stats[:, 38:39], in_=stats[:, 37:38]), deps=[g9])
                        g11 = opn("vector", lambda v: v.tensor_scalar(
                            out=stats[:, 39:40], in0=stats[:, 38:39],
                            scalar1=0.5, scalar2=None, op0=ALU.mult), deps=[g10])
                        g12 = opn("vector", lambda v, cj=cj: v.tensor_scalar(
                            out=stats[:, 40:48], in0=cj, scalar1=stats[:, 36:37],
                            scalar2=None, op0=ALU.is_ge), deps=[g7, g11])
                        g13 = opn("vector", lambda v, cj=cj: v.tensor_tensor(
                            out=cj, in0=cj, in1=stats[:, 40:48], op=ALU.mult),
                            deps=[g12])
                        dt2 = opn("vector", lambda v, cj=cj: v.tensor_scalar(
                            out=cj, in0=cj, scalar1=stats[:, 39:40],
                            scalar2=None, op0=ALU.mult), deps=[g13])
                        res["stats"] = dt2
                        est["comb"][j] = dt2
                    for e in range(E):
                        for phase_i, wsrc in enumerate([wgb, wub]):
                            for gi in range(3):
                                sl_i = gi % 2
                                slot = (wga, wgb_s)[sl_i]
                                t_wl = opn("gpsimd", lambda g, e=e, gi=gi, wsrc=wsrc,
                                           slot=slot: g.dma_start(
                                    out=slot[:].rearrange("p (c n) -> p c n", c=DC),
                                    in_=wsrc[e, gi].rearrange("(c p) n -> p c n", p=128)),
                                    deps=[est["wsl"][sl_i]], dma=True)
                                ni_g = (4, 4, 3)[gi]
                                last_mm = None
                                for il in range(ni_g):
                                    ig = gi * 4 + il
                                    for tc2 in range(2):
                                        bank = (il * 2 + tc2) % 4

                                        def mm(t, il=il, tc2=tc2, bank=bank, slot=slot):
                                            for dc in range(DC):
                                                m = t.matmul(
                                                    pb[bank][:],
                                                    slot[:, dc * 512 + il * 128:
                                                         dc * 512 + (il + 1) * 128],
                                                    xT[:, dc * TC + tc2 * 512:
                                                       dc * TC + (tc2 + 1) * 512],
                                                    start=(dc == 0), stop=(dc == DC - 1))
                                            return m
                                        dmm = opn("tensor", mm,
                                                  deps=[t_wl, res["pb"][bank]])
                                        est["xT_last"] = dmm
                                        last_mm = dmm
                                        sl = xeT[:, ig * TC + tc2 * 512:
                                                 ig * TC + (tc2 + 1) * 512]
                                        if phase_i == 0:
                                            dev = opn("scalar", lambda s, sl=sl,
                                                      bank=bank: s.activation(
                                                out=sl, in_=pb[bank][:], func=AF.Silu),
                                                deps=[dmm, est["xeT_r"]])
                                        else:
                                            dev = opn("vector", lambda v, sl=sl,
                                                      bank=bank: v.tensor_tensor(
                                                out=sl, in0=sl, in1=pb[bank][:],
                                                op=ALU.mult),
                                                deps=[dmm, est["xeT_w"][ig][tc2]])
                                        est["xeT_w"][ig][tc2] = dev
                                        est["xeT_last"] = dev
                                        res["pb"][bank] = dev
                                est["wsl"][sl_i] = last_mm
                        for dcb in range(4):
                            sl_i = dcb % 2
                            dslot = (wda, wdb_s)[sl_i]
                            t_wl = opn("gpsimd", lambda g, e=e, dcb=dcb, dslot=dslot:
                                       g.dma_start(
                                out=dslot[:].rearrange("p (c n) -> p c n", c=NI),
                                in_=wdb[e, dcb].rearrange("(c p) n -> p c n", p=128)),
                                deps=[est["dsl"][sl_i]], dma=True)
                            last_mm = None
                            xeT_deps = [est["xeT_w"][ig][tc2] for ig in range(NI)
                                        for tc2 in range(2)]
                            for j in range(NT):
                                bank = j % 4

                                def mm(t, j=j, bank=bank, dslot=dslot):
                                    for il in range(NI):
                                        m = t.matmul(pb[bank][:],
                                                     xeT[:, il * TC + j * 128:
                                                         il * TC + (j + 1) * 128],
                                                     dslot[:, il * 512:(il + 1) * 512],
                                                     start=(il == 0), stop=(il == NI - 1))
                                    return m
                                dmm = opn("tensor", mm,
                                          deps=[t_wl, res["pb"][bank]] + xeT_deps)
                                last_mm = dmm
                                est["xeT_r"] = dmm
                                dup = opn("vector", lambda v, j=j, dcb=dcb, e=e,
                                          bank=bank: v.scalar_tensor_tensor(
                                    out=x_sb[:, j * D + dcb * 512: j * D + (dcb + 1) * 512],
                                    in0=pb[bank][:],
                                    scalar=comb[:, j * E + e: j * E + e + 1],
                                    in1=x_sb[:, j * D + dcb * 512: j * D + (dcb + 1) * 512],
                                    op0=ALU.mult, op1=ALU.add),
                                    deps=[dmm, res["x"][j], est["comb"][j]])
                                res["x"][j] = dup
                                res["pb"][bank] = dup
                            est["dsl"][sl_i] = last_mm
                    res["xT_r"] = est["xT_last"]
                    sch.flush()

        def attn_block():
            sch.barrier()
            with contextlib.ExitStack() as sc_blk:
                sbb = lambda nm, shp, dt: sc_blk.enter_context(
                    nc.sbuf_tensor(f"{nm}_s2", shp, dt))
                hT = sbb("hT", [128, DC * TC], BF16)
                qs = sbb("qs", [128, NT * D], BF16)
                scr1 = sbb("scr1", [128, 512], F32)
                scr2 = sbb("scr2", [128, 512], F32)
                scr3 = sbb("scr3", [128, 512], F32)
                scr4 = sbb("scr4", [128, 512], F32)
                bbc = sbb("bbc", [128, 512], F32)
                r2cs = sbb("r2cs", [128, 512], F32)
                r2ss = sbb("r2ss", [128, 512], F32)
                rK = sbb("rK", [128, NT], F32)
                t_hT = [None] * (NT * DC)
                s2 = {"bbc_r": None, "scr_r": None, "r2_r": None, "hT_r": None,
                      "qs": {}, "kst": {}, "rk": [None] * NT}

                with contextlib.ExitStack() as sc_ph:
                    hs = sc_ph.enter_context(nc.sbuf_tensor("hs_s2", [128, D], BF16))
                    t_hs_r = None
                    for j in range(NT):
                        xj = x_sb[:, j * D:(j + 1) * D]
                        dsq = opn("scalar", lambda s, xj=xj: s.activation(
                            out=hs[:], in_=xj, func=AF.Square, accum_out=stats[:, 0:1]),
                            deps=[res["x"][j], t_hs_r, res["stats"]])
                        dsm = opn("vector", lambda v, xj=xj: v.tensor_reduce(
                            out=stats[:, 1:2], in_=xj, axis=AX.X, op=ALU.add,
                            negate=True), deps=[res["x"][j], res["stats"]])
                        dv1 = opn("vector", lambda v: v.tensor_scalar(
                            out=stats[:, 2:3], in0=stats[:, 1:2],
                            scalar1=1.0 / D, scalar2=None, op0=ALU.mult), deps=[dsm])
                        dv1b = opn("vector", lambda v: v.tensor_tensor(
                            out=stats[:, 6:7], in0=stats[:, 2:3],
                            in1=stats[:, 2:3], op=ALU.mult), deps=[dv1])
                        dvar = opn("vector", lambda v: v.scalar_tensor_tensor(
                            out=stats[:, 3:4], in0=stats[:, 0:1], scalar=1.0 / D,
                            in1=stats[:, 6:7], op0=ALU.mult, op1=ALU.subtract),
                            deps=[dsq, dv1b])
                        dr1a = opn("scalar", lambda s: s.activation(
                            out=stats[:, 4:5], in_=stats[:, 3:4], func=AF.Ln,
                            bias=cst[:, 0:1], scale=1.0), deps=[dvar])
                        dr1 = opn("scalar", lambda s: s.activation(
                            out=stats[:, 5:6], in_=stats[:, 4:5],
                            func=AF.Exp, scale=cst[:, 2:3]), deps=[dr1a])
                        dv2a = opn("vector", lambda v: v.tensor_tensor(
                            out=stats[:, 6:7], in0=stats[:, 5:6],
                            in1=stats[:, 5:6], op=ALU.mult), deps=[dr1])
                        dv2 = opn("vector", lambda v: v.tensor_tensor(
                            out=stats[:, 7:8], in0=stats[:, 3:4],
                            in1=stats[:, 6:7], op=ALU.mult), deps=[dv2a])
                        dr2a = opn("scalar", lambda s: s.activation(
                            out=stats[:, 8:9], in_=stats[:, 7:8], func=AF.Ln,
                            bias=cst[:, 0:1], scale=1.0), deps=[dv2])
                        dr2 = opn("scalar", lambda s: s.activation(
                            out=stats[:, 9:10], in_=stats[:, 8:9],
                            func=AF.Exp, scale=cst[:, 2:3]), deps=[dr2a])
                        dhs0 = opn("vector", lambda v: v.tensor_tensor(
                            out=stats[:, 10:11], in0=stats[:, 5:6],
                            in1=stats[:, 9:10], op=ALU.mult), deps=[dr2])
                        dhs = opn("vector", lambda v, xj=xj: v.tensor_scalar(
                            out=hs[:], in0=xj, scalar1=stats[:, 2:3],
                            scalar2=stats[:, 10:11], op0=ALU.add,
                            op1=ALU.mult), deps=[dhs0, dsq])
                        dxa = opn("vector", lambda v: (
                            v.tensor_scalar(out=stats[:, 11:12], in0=stats[:, 5:6],
                                            scalar1=1.0, scalar2=None, op0=ALU.add),
                            v.tensor_tensor(out=stats[:, 12:13], in0=stats[:, 2:3],
                                            in1=stats[:, 5:6], op=ALU.mult))[-1],
                            deps=[dr1])
                        dxu = opn("vector", lambda v, xj=xj: v.tensor_scalar(
                            out=xj, in0=xj, scalar1=stats[:, 11:12],
                            scalar2=stats[:, 12:13], op0=ALU.mult,
                            op1=ALU.add), deps=[dxa, dhs])
                        res["x"][j] = dxu
                        res["stats"] = dxu
                        for g in range(4):
                            dcs = list(range(g * 4, g * 4 + 4))

                            def ev(v, pt, dcs=dcs, j=j):
                                for i, dc in enumerate(dcs):
                                    m = v.tensor_scalar(
                                        out=hT[:, dc * TC + j * 128:
                                               dc * TC + (j + 1) * 128],
                                        in0=pt[:, i * 128:(i + 1) * 128],
                                        scalar1=lnw_col[:, dc:dc + 1],
                                        scalar2=lnb_col[:, dc:dc + 1],
                                        op0=ALU.mult, op1=ALU.add)
                                return m
                            dev = tpose_group(
                                [hs[:, dc * 128:(dc + 1) * 128] for dc in dcs],
                                ev, [dhs])
                            for dc in dcs:
                                t_hT[dc * NT + j] = dev
                            t_hs_r = dev
                    sch.flush()

                def proj_pass(nb, wsrc_ap, bias_ap, consume, wsl, wsl_key):
                    t_w = opn("gpsimd", lambda g: g.dma_start(
                        out=wsl[:].rearrange("p (c n) -> p c n", c=DC),
                        in_=wsrc_ap.rearrange("(c p) n -> p c n", p=128)),
                        deps=[s2.get(wsl_key)], dma=True)
                    t_b = opn("sync", lambda s: s.dma_start(
                        out=rowb[0:1, 0:512], in_=bias_ap),
                        deps=[res.get("rowb_r")], dma=True)
                    t_bb = opn("tensor", lambda t: t.matmul(
                        pb[4][:], ones_rf[:], rowb[0:1, 0:512], start=True, stop=True),
                        deps=[t_b, res["pb"][4]])
                    res["rowb_r"] = t_bb
                    t_bbc = opn("vector", lambda v: v.tensor_copy(
                        out=bbc[:], in_=pb[4][:]), deps=[t_bb, s2["bbc_r"]])
                    res["pb"][4] = t_bbc
                    last = None
                    for j in range(NT):
                        bank = j % 3

                        def mm(t, j=j, bank=bank):
                            for dc in range(DC):
                                m = t.matmul(pb[bank][:],
                                             hT[:, dc * TC + j * 128: dc * TC + (j + 1) * 128],
                                             wsl[:, dc * 512:(dc + 1) * 512],
                                             start=(dc == 0), stop=(dc == DC - 1))
                            return m
                        dmm = opn("tensor", mm,
                                  deps=[t_w, res["pb"][bank]] +
                                       [t_hT[dc * NT + j] for dc in range(DC)])
                        last = dmm
                        s2["hT_r"] = dmm
                        consume(j, bank, dmm, t_bbc)
                    s2[wsl_key] = last

                with contextlib.ExitStack() as sc_ph:
                    sbp = lambda nm, shp, dt: sc_ph.enter_context(
                        nc.sbuf_tensor(f"{nm}_s2", shp, dt))
                    kst = sbp("kst", [128, NT * D], BF16)
                    wsl2 = sbp("wsl2", [128, DC * 512], BF16)
                    for pr in range(4):
                        def ck(j, bank, dmm, t_bbc, pr=pr):
                            dev = opn("vector", lambda v, j=j, bank=bank, pr=pr:
                                      v.scalar_tensor_tensor(
                                          out=kst[:, j * D + pr * 512:
                                                  j * D + (pr + 1) * 512],
                                          in0=pb[bank][:], scalar=1.0, in1=bbc[:],
                                          op0=ALU.mult, op1=ALU.add),
                                      deps=[dmm, t_bbc])
                            res["pb"][bank] = dev
                            s2["bbc_r"] = dev
                            s2["kst"][(j, pr)] = dev
                        s2.setdefault("bbc_r", None)
                        proj_pass(4 + pr, win[4 + pr], bin_[4 + pr], ck, wsl2, "wsl2_r")
                    for j in range(NT):
                        t_r2 = opn("sync", lambda s, j=j: [
                            s.dma_start(out=r2cs[:], in_=rope2c[j * 128:(j + 1) * 128, :]),
                            s.dma_start(out=r2ss[:],
                                        in_=rope2s[j * 128:(j + 1) * 128, :])],
                            deps=[s2["r2_r"]], dma=True, ndma=2)
                        k1 = kst[:, j * D: j * D + 512]
                        k2 = kst[:, j * D + 512: j * D + 1024]

                        def ropeA(v, k1=k1, k2=k2):
                            v.tensor_tensor(out=scr1[:], in0=k1, in1=r2cs[:], op=ALU.mult)
                            v.tensor_tensor(out=scr2[:], in0=k2, in1=r2ss[:], op=ALU.mult)
                            v.tensor_tensor(out=scr3[:], in0=k2, in1=r2cs[:], op=ALU.mult)
                            return v.tensor_tensor(out=scr4[:], in0=k1, in1=r2ss[:],
                                                   op=ALU.mult)
                        drpa = opn("vector", ropeA,
                                   deps=[t_r2, s2["kst"][(j, 0)], s2["kst"][(j, 1)],
                                         s2["scr_r"]])

                        def ropeB(v, k1=k1, k2=k2):
                            v.tensor_tensor(out=k1, in0=scr1[:], in1=scr2[:],
                                            op=ALU.subtract)
                            return v.tensor_tensor(out=k2, in0=scr3[:], in1=scr4[:],
                                                   op=ALU.add)
                        drp = opn("vector", ropeB, deps=[drpa])
                        s2["r2_r"] = drp
                        s2["scr_r"] = drp

                        def ksq(s, j=j):
                            scrs = [scr1, scr2, scr3, scr4]
                            for pr in range(4):
                                m = s.activation(out=scrs[pr][:],
                                                 in_=kst[:, j * D + pr * 512:
                                                        j * D + (pr + 1) * 512],
                                                 func=AF.Square,
                                                 accum_out=stats[:, 16 + pr:17 + pr])
                            return m
                        dq = opn("scalar", ksq,
                                 deps=[drp, s2["kst"][(j, 2)], s2["kst"][(j, 3)],
                                       res["stats"]])
                        s2["scr_r"] = dq
                        dsumv = opn("vector", lambda v: v.tensor_reduce(
                            out=stats[:, 20:21], in_=stats[:, 16:20], axis=AX.X,
                            op=ALU.add), deps=[dq])
                        drka = opn("scalar", lambda s: s.activation(
                            out=stats[:, 21:22], in_=stats[:, 20:21],
                            func=AF.Ln), deps=[dsumv])
                        drk = opn("scalar", lambda s, j=j: s.activation(
                            out=rK[:, j:j + 1], in_=stats[:, 21:22],
                            func=AF.Exp, scale=cst[:, 2:3]), deps=[drka])
                        res["stats"] = drk
                        s2["rk"][j] = drk
                    dst_a = []
                    for pr in range(4):
                        a_last = [None]

                        def cv(j, bank, dmm, t_bbc, pr=pr, a_last=a_last):
                            dv = opn("vector", lambda v, bank=bank:
                                     v.scalar_tensor_tensor(
                                         out=scr2[:], in0=pb[bank][:], scalar=1.0,
                                         in1=bbc[:], op0=ALU.mult, op1=ALU.add),
                                     deps=[dmm, t_bbc, s2["scr_r"]])
                            res["pb"][bank] = dv
                            s2["bbc_r"] = dv
                            dkv = opn("vector", lambda v, j=j, pr=pr:
                                      v.scalar_tensor_tensor(
                                          out=scr3[:],
                                          in0=kst[:, j * D + pr * 512:
                                                  j * D + (pr + 1) * 512],
                                          scalar=rK[:, j:j + 1], in1=scr2[:],
                                          op0=ALU.mult, op1=ALU.mult),
                                      deps=[dv, s2["rk"][j]])
                            dA = opn("tensor", lambda t, j=j: t.matmul(
                                pmx[0:1, 0:512], ones_cf[:], scr3[:],
                                start=(j == 0), stop=(j == NT - 1)),
                                deps=[dkv] + ([res["pmx"]] if j == 0 else []))
                            s2["scr_r"] = dA
                            a_last[0] = dA
                        proj_pass(8 + pr, win[8 + pr], bin_[8 + pr], cv, wsl2, "wsl2_r")
                        dcp = opn("vector", lambda v: v.tensor_copy(
                            out=rowb[0:1, 512:1024], in_=pmx[0:1, 0:512]),
                            deps=[a_last[0], res.get("rowb_r")])
                        res["pmx"] = dcp
                        dsa = opn("sync", lambda s, pr=pr: s.dma_start(
                            out=a_in[0:1, pr * 512:(pr + 1) * 512],
                            in_=rowb[0:1, 512:1024]), deps=[dcp], dma=True)
                        res["rowb_r"] = dsa
                        dst_a.append(dsa)
                    t_ccA = opn("gpsimd", lambda g: g.collective_compute(
                        "AllReduce", ALU.add, replica_groups=RG,
                        ins=[a_in.ap().opt()], outs=[a_out.ap().opt()]), deps=dst_a)
                    sch.flush()

                sch.barrier()
                with contextlib.ExitStack() as sc_ph:
                    sbp = lambda nm, shp, dt: sc_ph.enter_context(
                        nc.sbuf_tensor(f"{nm}_s2", shp, dt))
                    wslq = sbp("wslq", [128, DC * 512], BF16)
                    abc = sbp("abc", [128, D], F32)
                    for nb in range(4):
                        def cq(j, bank, dmm, t_bbc, nb=nb):
                            dev = opn("vector", lambda v, j=j, bank=bank, nb=nb:
                                      v.scalar_tensor_tensor(
                                          out=qs[:, j * D + nb * 512:
                                                 j * D + (nb + 1) * 512],
                                          in0=pb[bank][:], scalar=1.0, in1=bbc[:],
                                          op0=ALU.mult, op1=ALU.add),
                                      deps=[dmm, t_bbc])
                            res["pb"][bank] = dev
                            s2["bbc_r"] = dev
                            s2["qs"][(j, nb)] = dev
                        proj_pass(nb, win[nb], bin_[nb], cq, wslq, "wslq_r")
                    for j in range(NT):
                        t_r2 = opn("sync", lambda s, j=j: [
                            s.dma_start(out=r2cs[:], in_=rope2c[j * 128:(j + 1) * 128, :]),
                            s.dma_start(out=r2ss[:],
                                        in_=rope2s[j * 128:(j + 1) * 128, :])],
                            deps=[s2["r2_r"]], dma=True, ndma=2)
                        q1 = qs[:, j * D: j * D + 512]
                        q2 = qs[:, j * D + 512: j * D + 1024]

                        def ropeA(v, q1=q1, q2=q2):
                            v.tensor_tensor(out=scr1[:], in0=q1, in1=r2cs[:], op=ALU.mult)
                            v.tensor_tensor(out=scr2[:], in0=q2, in1=r2ss[:], op=ALU.mult)
                            v.tensor_tensor(out=scr3[:], in0=q2, in1=r2cs[:], op=ALU.mult)
                            return v.tensor_tensor(out=scr4[:], in0=q1, in1=r2ss[:],
                                                   op=ALU.mult)
                        drpa = opn("vector", ropeA,
                                   deps=[t_r2, s2["qs"][(j, 0)], s2["qs"][(j, 1)],
                                         s2["scr_r"]])

                        def ropeB(v, q1=q1, q2=q2):
                            v.tensor_tensor(out=q1, in0=scr1[:], in1=scr2[:],
                                            op=ALU.subtract)
                            return v.tensor_tensor(out=q2, in0=scr3[:], in1=scr4[:],
                                                   op=ALU.add)
                        drp = opn("vector", ropeB, deps=[drpa])
                        s2["r2_r"] = drp
                        s2["scr_r"] = drp
                        s2["qs"][(j, "rope")] = drp
                    t_al = opn("sync", lambda s: s.dma_start(
                        out=rowb[0:1, 0:1024], in_=a_out[0:1, 0:1024]),
                        deps=[t_ccA, res.get("rowb_r")], dma=True)
                    dabc = None
                    for hh in range(2):
                        tbx = opn("tensor", lambda t, hh=hh: t.matmul(
                            pb[4][:], ones_rf[:], rowb[0:1, hh * 512:(hh + 1) * 512],
                            start=True, stop=True), deps=[t_al, res["pb"][4]])
                        dabc = opn("vector", lambda v, hh=hh: v.tensor_copy(
                            out=abc[:, hh * 512:(hh + 1) * 512], in_=pb[4][:]),
                            deps=[tbx])
                        res["pb"][4] = dabc
                    t_al2 = opn("sync", lambda s: s.dma_start(
                        out=rowb[0:1, 0:1024], in_=a_out[0:1, 1024:2048]),
                        deps=[dabc, t_ccA], dma=True)
                    res["rowb_r"] = t_al2
                    for hh in range(2):
                        tbx = opn("tensor", lambda t, hh=hh: t.matmul(
                            pb[4][:], ones_rf[:], rowb[0:1, hh * 512:(hh + 1) * 512],
                            start=True, stop=True), deps=[t_al2, res["pb"][4]])
                        dabc = opn("vector", lambda v, hh=hh: v.tensor_copy(
                            out=abc[:, 1024 + hh * 512: 1024 + (hh + 1) * 512],
                            in_=pb[4][:]), deps=[tbx])
                        res["pb"][4] = dabc
                    res["rowb_r"] = dabc
                    t_YT = [None] * (NT * DC)
                    with contextlib.ExitStack() as sc_y:
                        ys = sc_y.enter_context(
                            nc.sbuf_tensor("ys_s2", [128, D], BF16))
                        t_ys_r = None
                        for j in range(NT):
                            dY = opn("vector", lambda v, j=j: v.tensor_tensor(
                                out=ys[:], in0=qs[:, j * D:(j + 1) * D], in1=abc[:],
                                op=ALU.mult),
                                deps=[s2["qs"][(j, "rope")], s2["qs"][(j, 2)],
                                      s2["qs"][(j, 3)], dabc, t_ys_r])
                            for g in range(4):
                                dcs = list(range(g * 4, g * 4 + 4))

                                def ev(v, pt, dcs=dcs, j=j):
                                    for i, dc in enumerate(dcs):
                                        m = v.tensor_copy(
                                            out=hT[:, dc * TC + j * 128:
                                                   dc * TC + (j + 1) * 128],
                                            in_=pt[:, i * 128:(i + 1) * 128])
                                    return m
                                dev = tpose_group(
                                    [ys[:, dc * 128:(dc + 1) * 128] for dc in dcs],
                                    ev, [dY, s2["hT_r"]])
                                for dc in dcs:
                                    t_YT[dc * NT + j] = dev
                                t_ys_r = dev
                        sch.flush()
                    for nb in range(4):
                        t_w = opn("gpsimd", lambda g, nb=nb: g.dma_start(
                            out=wslq[:].rearrange("p (c n) -> p c n", c=DC),
                            in_=wout[nb].rearrange("(c p) n -> p c n", p=128)),
                            deps=[s2.get("wslq_r")], dma=True)
                        t_b = opn("sync", lambda s, nb=nb: s.dma_start(
                            out=rowb[0:1, 0:512], in_=bout[nb]),
                            deps=[res.get("rowb_r")], dma=True)
                        t_bb = opn("tensor", lambda t: t.matmul(
                            pb[4][:], ones_rf[:], rowb[0:1, 0:512], start=True,
                            stop=True), deps=[t_b, res["pb"][4]])
                        res["rowb_r"] = t_bb
                        t_bbc = opn("vector", lambda v: v.tensor_copy(
                            out=bbc[:], in_=pb[4][:]), deps=[t_bb, s2["bbc_r"]])
                        res["pb"][4] = t_bbc
                        last = None
                        for j in range(NT):
                            bank = j % 3

                            def mm(t, j=j, bank=bank):
                                for dc in range(DC):
                                    m = t.matmul(pb[bank][:],
                                                 hT[:, dc * TC + j * 128:
                                                    dc * TC + (j + 1) * 128],
                                                 wslq[:, dc * 512:(dc + 1) * 512],
                                                 start=(dc == 0), stop=(dc == DC - 1))
                                return m
                            dmm = opn("tensor", mm,
                                      deps=[t_w, res["pb"][bank]] +
                                           [t_YT[dc * NT + j] for dc in range(DC)])
                            last = dmm
                            dup0 = opn("vector", lambda v, bank=bank: v.tensor_tensor(
                                out=scr1[:], in0=pb[bank][:], in1=bbc[:], op=ALU.add),
                                deps=[dmm, t_bbc, s2["scr_r"]])
                            dup = opn("vector", lambda v, j=j, nb=nb: v.tensor_tensor(
                                out=x_sb[:, j * D + nb * 512: j * D + (nb + 1) * 512],
                                in0=x_sb[:, j * D + nb * 512: j * D + (nb + 1) * 512],
                                in1=scr1[:], op=ALU.add),
                                deps=[dup0, res["x"][j]])
                            res["x"][j] = dup
                            s2["scr_r"] = dup
                            s2["bbc_r"] = dup
                            res["pb"][bank] = dup
                        s2["wslq_r"] = last
                    sch.flush()

        ffn_block("s1")
        if upto not in ("a1", "a2", "attn", "oproj", "s1"):
            attn_block()
            if upto != "s2":
                ffn_block("s3")

        t_st = opn("sync", lambda s: s.dma_start(
            out=y_out.rearrange("(t p) d -> p t d", p=128),
            in_=x_sb[:].rearrange("p (t d) -> p t d", t=NT)),
            deps=[res["x"][j] for j in range(NT)], dma=True)
        lasts = sch.last()
        opn("sync", lambda s: s.dma_start(out=y_out[0:1, 0:1], in_=x_sb[0:1, 0:1]),
            deps=[t_st] + lasts, dma=True)
        finals = [(k, sch.cnt[k]) for k in kernel_dma_keys_sync()]
        sch.flush()
        sch.block.sync(lambda h: [h.wait_ge(sch.sem[k], v) for k, v in finals])

    return nc



_CACHE = {}


def _prep(inputs):
    f32 = np.float32
    X = np.asarray(inputs["L0_attn_X"], f32).reshape(S, D)

    qw = np.asarray(inputs["L0_ffn_q_w"], f32)
    kw = np.asarray(inputs["L0_ffn_k_w"], f32)
    vw = np.asarray(inputs["L0_ffn_v_w"], f32)
    ow = np.asarray(inputs["L0_ffn_o_w"], f32)
    wq_h = np.ascontiguousarray(qw.T.reshape(D, 4, 512).transpose(1, 0, 2))
    wk_h = np.ascontiguousarray(kw.T)
    wv_h = np.ascontiguousarray(vw.T)
    wo_h = np.ascontiguousarray(ow.T.reshape(D, 4, 512).transpose(1, 0, 2))
    wgate_h = np.ascontiguousarray(np.asarray(inputs["L0_ffn_gate_w"], f32).T)
    wg = np.asarray(inputs["L0_ffn_wg"], f32)
    wu = np.asarray(inputs["L0_ffn_wu"], f32)
    wd = np.asarray(inputs["L0_ffn_wd"], f32)
    wgb_h = np.zeros((E, 3, D, 512), f32)
    wub_h = np.zeros((E, 3, D, 512), f32)
    for e in range(E):
        wgt = wg[e].T
        wut = wu[e].T
        for gi, (lo, hi) in enumerate([(0, 512), (512, 1024), (1024, 1408)]):
            wgb_h[e, gi, :, :hi - lo] = wgt[:, lo:hi]
            wub_h[e, gi, :, :hi - lo] = wut[:, lo:hi]
    wdb_h = np.zeros((E, 4, IFF, 512), f32)
    for e in range(E):
        wdt = wd[e].T
        for dcb in range(4):
            wdb_h[e, dcb] = wdt[:, dcb * 512:(dcb + 1) * 512]

    perm = np.concatenate([np.arange(0, 1024, 2), np.arange(1, 1024, 2),
                           np.arange(1024, 2048)])
    Win = np.asarray(inputs["L0_attn_W_in"], f32)
    bin0 = np.asarray(inputs["L0_attn_b_in"], f32)
    Winp = np.concatenate([Win[:, 0:2048][:, perm], Win[:, 2048:4096][:, perm],
                           Win[:, 4096:6144][:, perm]], axis=1)
    binp = np.concatenate([bin0[0:2048][perm], bin0[2048:4096][perm],
                           bin0[4096:6144][perm]])
    win_h = np.ascontiguousarray(Winp.reshape(D, 12, 512).transpose(1, 0, 2))
    bin_h = np.ascontiguousarray(binp.reshape(12, 1, 512))
    Woutp = np.asarray(inputs["L0_attn_W_out"], f32)[perm, :]
    wout_h = np.ascontiguousarray(Woutp.reshape(D, 4, 512).transpose(1, 0, 2))
    bout_h = np.ascontiguousarray(np.asarray(inputs["L0_attn_b_out"], f32)
                                  .reshape(4, 1, 512))

    qq = np.arange(128)[:, None]
    maska_h = np.zeros((4, 128, 512), f32)
    masktA_h = np.zeros((4, 128, 512), f32)
    kk = np.arange(512)[None, :]
    for t in range(4):
        maska_h[t] = np.where(kk <= (t * 128 + qq), 0.0, -1e30)
    for kl in range(4):
        kp = np.arange(128)[:, None]
        qcol = np.arange(512)[None, :]
        allowed = (kl * 128 + kp) <= ((qcol // 128) * 128 + (qcol % 128))
        masktA_h[kl] = np.where(allowed, 0.0, -1e30)

    pos_all = np.arange(S, dtype=np.float64)
    f1 = np.power(1e6, -np.arange(0, HD, 2, dtype=np.float64) / HD)
    ang1 = pos_all[:, None] * f1[None, :]
    r1c_full = np.cos(ang1).astype(f32)
    r1s_full = np.sin(ang1).astype(f32)
    f2 = np.power(1e4, -np.arange(0, 1024, 2, dtype=np.float64) / 1024)
    ang2 = pos_all[:, None] * f2[None, :]
    r2c_full = np.cos(ang2).astype(f32)
    r2s_full = np.sin(ang2).astype(f32)

    shared = dict(
        wq=wq_h, wk=wk_h, wv=wv_h, wo=wo_h, wgate=wgate_h,
        wgb=wgb_h, wub=wub_h, wdb=wdb_h,
        nw_attn=np.ascontiguousarray(
            np.asarray(inputs["L0_ffn_attn_norm_w"], f32).reshape(DC, 128).T),
        nw_ffn=np.ascontiguousarray(
            np.asarray(inputs["L0_ffn_ffn_norm_w"], f32).reshape(DC, 128).T),
        nw_q=np.asarray(inputs["L0_ffn_q_norm_w"], f32).reshape(1, HD),
        nw_k=np.asarray(inputs["L0_ffn_k_norm_w"], f32).reshape(1, HD),
        win=win_h, bin=bin_h, wout=wout_h, bout=bout_h,
        lnw=np.ascontiguousarray(
            np.asarray(inputs["L0_attn_ln_w"], f32).reshape(DC, 128).T),
        lnb=np.ascontiguousarray(
            np.asarray(inputs["L0_attn_ln_b"], f32).reshape(DC, 128).T),
        maska=maska_h, masktA=masktA_h,
    )
    in_maps = []
    pp = np.arange(128, dtype=np.uint32)
    for c in range(NCORE):
        m = dict(shared)
        idx = np.zeros((48, 128), np.uint32)
        g = c // 2
        for i in range(NCORE):
            idx[0 + i] = i * 2048 + (2 * c) * 128 + pp
            idx[8 + i] = i * 2048 + (2 * c + 1) * 128 + pp
            idx[16 + i] = i * 1024 + g * 128 + pp
            idx[24 + i] = i * 1024 + 512 + g * 128 + pp
        for r in range(NCORE):
            for hl in range(2):
                idx[32 + r * 2 + hl] = r * 2048 + c * 256 + hl * 128 + pp
        m["idxs"] = np.ascontiguousarray(idx.T)
        m["x_in"] = np.ascontiguousarray(X[c * TC:(c + 1) * TC])
        m["rope1c"] = np.ascontiguousarray(r1c_full[c * TC:(c + 1) * TC])
        m["rope1s"] = np.ascontiguousarray(r1s_full[c * TC:(c + 1) * TC])
        m["rope2c"] = np.ascontiguousarray(r2c_full[c * TC:(c + 1) * TC])
        m["rope2s"] = np.ascontiguousarray(r2s_full[c * TC:(c + 1) * TC])
        in_maps.append(m)
    return in_maps


def kernel(**inputs):
    upto = inputs.pop("_upto", "all")
    if upto not in _CACHE:
        _CACHE[upto] = build_program(upto)
    nc = _CACHE[upto]
    in_maps = _prep(inputs)
    res = run_bass_kernel_spmd(nc, in_maps, core_ids=list(range(NCORE)))
    out = np.concatenate([res.results[c]["y_out"] for c in range(NCORE)], axis=0)
    return out.reshape(1, S, D).astype(np.float32)


# version 09
# speedup vs baseline: 1.0165x; 1.0165x over previous
import contextlib
import numpy as np

import concourse.bass as bass
import concourse.mybir as mybir
from concourse.bass_utils import run_bass_kernel_spmd

F32 = mybir.dt.float32
BF16 = mybir.dt.bfloat16
AF = mybir.ActivationFunctionType
ALU = mybir.AluOpType
AX = mybir.AxisListType

NCORE = 8
D = 2048
S = 8192
TC = S // NCORE
NT = TC // 128
HD = 128
E, IFF = 8, 1408
NI = IFF // 128
DC = D // 128
LN_EPS = 1e-5
RMS_EPS = 1e-6

def kernel_dma_keys_sync():
    return SY_DMA_KEYS


ENGINES = ("tensor", "vector", "scalar", "gpsimd", "sync")
GP_DMA_KEYS = tuple(f"gpd{i}" for i in range(4))
SY_DMA_KEYS = tuple(f"syd{i}" for i in range(2))
DMA_KEYS = GP_DMA_KEYS + SY_DMA_KEYS
SEMKEYS = ("tensor", "vector", "scalar", "gpsimd") + DMA_KEYS


class Tok:
    __slots__ = ("eng", "cnt")

    def __init__(self, eng, cnt):
        self.eng, self.cnt = eng, cnt


class Sched:

    def __init__(self, nc):
        self.nc = nc
        self.block = None
        self.sem = {}
        self.cnt = {e: 0 for e in SEMKEYS}
        self.hw = {e: {o: 0 for o in SEMKEYS} for e in ENGINES}
        self.pend = {e: [] for e in ENGINES}
        self.bar = {e: [] for e in ENGINES}
        self.awaited = {k: 0 for k in DMA_KEYS}
        self.qwait = {k: 0 for k in DMA_KEYS}
        self.rr = {"gpsimd": 0, "sync": 0}

    def init_sems(self, ctx):
        for e in SEMKEYS:
            self.sem[e] = ctx.enter_context(self.nc.semaphore(f"pg_{e}"))

    def barrier(self):
        toks = [Tok(e, self.cnt[e]) for e in SEMKEYS if self.cnt[e] > 0]
        for e in ENGINES:
            self.bar[e] = list(toks)

    def op(self, eng, fn, deps=(), dma=False, ndma=1):
        if dma:
            pool = GP_DMA_KEYS if eng == "gpsimd" else SY_DMA_KEYS
            key = pool[self.rr[eng] % len(pool)]
            self.rr[eng] += 1
        else:
            key = eng
        n = 16 * ndma if dma else 1
        alldeps = list(deps) + self.bar[eng]
        self.bar[eng] = []
        waits = []
        for d in alldeps:
            if d is None:
                continue
            if d.eng == key == "tensor":
                continue
            if d.eng in DMA_KEYS:
                want = self.cnt[d.eng]
                self.awaited[d.eng] = max(self.awaited[d.eng], want)
            else:
                want = d.cnt
            if want > self.hw[eng][d.eng]:
                self.hw[eng][d.eng] = want
                waits.append((d.eng, want))
        if dma and self.awaited.get(key, 0) > self.qwait[key]:
            self.qwait[key] = self.awaited[key]
            waits.append((key, self.qwait[key]))
        self.cnt[key] += n
        tok = Tok(key, self.cnt[key])
        self.pend[eng].append((waits, fn, n, dma, key))
        return tok

    def flush(self):
        for e in ENGINES:
            items = self.pend[e]
            if not items:
                continue
            self.pend[e] = []
            sems = self.sem

            def body(h, items=items, e=e):
                for waits, fn, n, dma, key in items:
                    for weng, wcnt in waits:
                        h.wait_ge(sems[weng], wcnt)
                    r = fn(h)
                    if dma:
                        instrs = r if isinstance(r, (list, tuple)) else [r]
                        assert len(instrs) * 16 == n, (n, len(instrs))
                        for ins in instrs:
                            ins.then_inc(sems[key], 16)
                    else:
                        r.then_inc(sems[key], n)
            getattr(self.block, e)(body)

    def last(self):
        return [Tok(e, self.cnt[e]) for e in SEMKEYS if self.cnt[e] > 0]


def build_program(upto="all"):
    nc = bass.Bass("TRN2", target_bir_lowering=False)

    x_in = nc.dram_tensor("x_in", [TC, D], F32, kind="ExternalInput").ap()
    y_out = nc.dram_tensor("y_out", [TC, D], F32, kind="ExternalOutput").ap()

    wq = nc.dram_tensor("wq", [4, D, 512], F32, kind="ExternalInput").ap()
    wk = nc.dram_tensor("wk", [D, 512], F32, kind="ExternalInput").ap()
    wv = nc.dram_tensor("wv", [D, 512], F32, kind="ExternalInput").ap()
    wo = nc.dram_tensor("wo", [4, D, 512], F32, kind="ExternalInput").ap()
    wgate = nc.dram_tensor("wgate", [D, E], F32, kind="ExternalInput").ap()
    wgb = nc.dram_tensor("wgb", [E, 3, D, 512], F32, kind="ExternalInput").ap()
    wub = nc.dram_tensor("wub", [E, 3, D, 512], F32, kind="ExternalInput").ap()
    wdb = nc.dram_tensor("wdb", [E, 4, IFF, 512], F32, kind="ExternalInput").ap()
    nw_attn = nc.dram_tensor("nw_attn", [128, DC], F32, kind="ExternalInput").ap()
    nw_ffn = nc.dram_tensor("nw_ffn", [128, DC], F32, kind="ExternalInput").ap()
    nw_q = nc.dram_tensor("nw_q", [1, HD], F32, kind="ExternalInput").ap()
    nw_k = nc.dram_tensor("nw_k", [1, HD], F32, kind="ExternalInput").ap()

    win = nc.dram_tensor("win", [12, D, 512], F32, kind="ExternalInput").ap()
    bin_ = nc.dram_tensor("bin", [12, 1, 512], F32, kind="ExternalInput").ap()
    wout = nc.dram_tensor("wout", [4, D, 512], F32, kind="ExternalInput").ap()
    bout = nc.dram_tensor("bout", [4, 1, 512], F32, kind="ExternalInput").ap()
    lnw = nc.dram_tensor("lnw", [128, DC], F32, kind="ExternalInput").ap()
    lnb = nc.dram_tensor("lnb", [128, DC], F32, kind="ExternalInput").ap()

    rope1c = nc.dram_tensor("rope1c", [TC, 64], F32, kind="ExternalInput").ap()
    rope1s = nc.dram_tensor("rope1s", [TC, 64], F32, kind="ExternalInput").ap()
    rope2c = nc.dram_tensor("rope2c", [TC, 512], F32, kind="ExternalInput").ap()
    rope2s = nc.dram_tensor("rope2s", [TC, 512], F32, kind="ExternalInput").ap()
    maska = nc.dram_tensor("maska", [4, 128, 512], F32, kind="ExternalInput").ap()
    masktA = nc.dram_tensor("masktA", [4, 128, 512], F32, kind="ExternalInput").ap()

    q_in = nc.dram_tensor("q_in", [2048, TC], BF16)
    q_out = nc.dram_tensor("q_out", [NCORE * 2048, TC], BF16)
    kv_in = nc.dram_tensor("kv_in", [1024, TC], BF16)
    kv_out = nc.dram_tensor("kv_out", [NCORE * 1024, TC], BF16)
    at_in = nc.dram_tensor("at_in", [2048, TC], BF16)
    at_out = nc.dram_tensor("at_out", [NCORE * 2048, TC], BF16)
    idxs = nc.dram_tensor("idxs", [128, 48], mybir.dt.uint32,
                          kind="ExternalInput").ap()
    a_in = nc.dram_tensor("a_in", [1, D], F32)
    a_out = nc.dram_tensor("a_out", [1, D], F32)
    RG = [list(range(NCORE))]

    with contextlib.ExitStack() as top:
        sb = lambda nm, shp, dt: top.enter_context(nc.sbuf_tensor(nm, shp, dt))
        x_sb = sb("x_sb", [128, NT * D], F32)
        ones_r = sb("ones_r", [1, 128], BF16)
        ones_rf = sb("ones_rf", [1, 128], F32)
        ones_cf = sb("ones_cf", [128, 1], F32)
        ones_cb = sb("ones_cb", [128, 1], BF16)
        eye_b = sb("eye_b", [128, 128], BF16)
        eye_f = sb("eye_f", [128, 128], F32)
        nwq_bc = sb("nwq_bc", [128, 128], F32)
        nwk_bc = sb("nwk_bc", [128, 128], F32)
        nwa_col = sb("nwa_col", [128, DC], F32)
        nwf_col = sb("nwf_col", [128, DC], F32)
        lnw_col = sb("lnw_col", [128, DC], F32)
        lnb_col = sb("lnb_col", [128, DC], F32)
        gw_sb = sb("gw_sb", [128, DC * E], BF16)
        r1c = sb("r1c", [128, NT * 64], F32)
        r1s = sb("r1s", [128, NT * 64], F32)
        stats = sb("stats", [128, 96], F32)
        cst = sb("cst", [128, 4], F32)
        rowb = sb("rowb", [1, 1024], F32)
        nmrow = sb("nmrow", [1, 1024], BF16)
        idx_sb = sb("idx_sb", [128, 48], mybir.dt.uint32)
        pb = [top.enter_context(nc.psum_tensor(f"pb{i}", [128, 512], F32)) for i in range(5)]
        ptr_ab = [top.enter_context(nc.psum_tensor(f"ptr{i}", [128, 512], BF16))
                  for i in range(2)]
        pmx = top.enter_context(nc.psum_tensor("pmx", [128, 512], F32))

        sch = Sched(nc)
        sch.init_sems(top)
        sch.block = top.enter_context(nc.Block())
        opn = sch.op

        res = {"x": [None] * NT, "pb": [None] * 5, "ptrg": [None, None],
               "pmx": None, "stats": None}
        tb_state = {"bank": 0}
        sca = lambda ab, c: stats[:, ab + c:ab + c + 1]

        def tpose_group(srcs, emit_evac, deps_src):
            bank = tb_state["bank"]
            tb_state["bank"] ^= 1
            pt = ptr_ab[bank]

            def tr(t, srcs=srcs, pt=pt):
                for i, s in enumerate(srcs):
                    m = t.transpose(pt[:, i * 128:(i + 1) * 128], s, eye_b[:])
                return m
            dtr = opn("tensor", tr, deps=list(deps_src) + [res["ptrg"][bank]])
            dev = opn("vector", lambda v, pt=pt: emit_evac(v, pt), deps=[dtr])
            res["ptrg"][bank] = dev
            return dev

        t_init0 = opn("gpsimd", lambda g: (
            g.memset(ones_r[:], 1.0), g.memset(ones_rf[:], 1.0),
            g.memset(ones_cf[:], 1.0), g.memset(ones_cb[:], 1.0),
            g.memset(cst[:, 0:1], LN_EPS), g.memset(cst[:, 1:2], RMS_EPS),
            g.memset(cst[:, 2:3], -0.5), g.memset(cst[:, 3:4], 1.0 / D),
            g.iota(eye_f[:], [[1, 128]], base=0, channel_multiplier=-1,
                   allow_small_or_imprecise_dtypes=True))[-1])
        t_init1 = opn("gpsimd", lambda g: g.tensor_scalar(
            out=eye_f[:], in0=eye_f[:], scalar1=0.0, scalar2=None,
            op0=ALU.is_equal), deps=[t_init0])
        t_init = opn("gpsimd", lambda g: g.tensor_copy(out=eye_b[:], in_=eye_f[:]),
                     deps=[t_init1])
        t_gw = opn("gpsimd", lambda g: g.dma_start(
            out=gw_sb[:].rearrange("p (c e) -> p c e", c=DC),
            in_=wgate.rearrange("(c p) e -> p c e", p=128)), dma=True)
        t_x = opn("sync", lambda s: s.dma_start(
            out=x_sb[:].rearrange("p (t d) -> p t d", t=NT),
            in_=x_in.rearrange("(t p) d -> p t d", p=128)), dma=True)
        t_cols = opn("sync", lambda s: [
            s.dma_start(out=nwa_col[:], in_=nw_attn),
            s.dma_start(out=nwf_col[:], in_=nw_ffn),
            s.dma_start(out=lnw_col[:], in_=lnw),
            s.dma_start(out=lnb_col[:], in_=lnb)], dma=True, ndma=4)
        t_idx = opn("sync", lambda s: s.dma_start(out=idx_sb[:], in_=idxs),
                    dma=True)
        t_rows = opn("sync", lambda s: [
            s.dma_start(out=rowb[0:1, 0:128], in_=nw_q),
            s.dma_start(out=rowb[0:1, 128:256], in_=nw_k)], dma=True, ndma=2)
        t_r1 = opn("sync", lambda s: [
            s.dma_start(out=r1c[:].rearrange("p (t k) -> p t k", t=NT),
                        in_=rope1c.rearrange("(t p) k -> p t k", p=128)),
            s.dma_start(out=r1s[:].rearrange("p (t k) -> p t k", t=NT),
                        in_=rope1s.rearrange("(t p) k -> p t k", p=128))], dma=True,
            ndma=2)
        t_b1 = opn("tensor", lambda t: t.matmul(pb[0][:, 0:128], ones_rf[:],
                                                rowb[0:1, 0:128], start=True, stop=True),
                   deps=[t_rows, t_cols, t_init])
        t_b2 = opn("tensor", lambda t: t.matmul(pb[1][:, 0:128], ones_rf[:],
                                                rowb[0:1, 128:256], start=True, stop=True),
                   deps=[t_rows])
        t_nbc = opn("vector", lambda v: (
            v.tensor_scalar(out=nwq_bc[:], in0=pb[0][:, 0:128],
                            scalar1=float(HD) ** -0.5, scalar2=None, op0=ALU.mult),
            v.tensor_copy(out=nwk_bc[:], in_=pb[1][:, 0:128]))[-1],
            deps=[t_b1, t_b2])
        res["pb"][0] = t_nbc
        res["pb"][1] = t_nbc
        res["rowb_r"] = t_b2

        def ffn_block(tag):
            sch.barrier()
            with contextlib.ExitStack() as sc_blk:
                sbb = lambda nm, shp, dt: sc_blk.enter_context(
                    nc.sbuf_tensor(f"{nm}_{tag}", shp, dt))
                xT = sbb("xT", [128, DC * TC], BF16)
                hid_stack = contextlib.ExitStack()
                sc_blk.callback(hid_stack.close)
                hid = hid_stack.enter_context(
                    nc.sbuf_tensor(f"hid_{tag}", [128, NT * D], BF16))
                t_hid = [None] * NT
                t_xT = [None] * (NT * DC)

                sc_a2 = contextlib.ExitStack()
                sba = lambda nm, shp, dt: sc_a2.enter_context(
                    nc.sbuf_tensor(f"{nm}_{tag}", shp, dt))
                wsl = sba("wsl", [128, 2 * DC * 512], BF16)
                ropeb = sba("ropeb", [128, 512], F32)
                qdmp = sba("qdmp", [128, 512], BF16)
                ropo = sba("ropo", [128, 1024], BF16)
                qst = sba("qst", [128, 1024], BF16)
                tmpa = sba("tmpa", [128, 256], F32)
                tmpb = sba("tmpb", [128, 256], F32)
                tmpc = sba("tmpc", [128, 256], F32)
                tmpd = sba("tmpd", [128, 256], F32)
                st = {"wsl_r": [None, None], "ropeb_w": None, "dump_r": None,
                      "tmp_r": None, "ropo_r": [None, None],
                      "qst_r": [[None] * 4 for _ in range(2)],
                      "stats": [None] * 4, "nbl": 0}

                def qkv_load(wsrc):
                    sl = st["nbl"] % 2
                    st["nbl"] += 1
                    wofs = sl * DC * 512
                    return opn("gpsimd", lambda g, wofs=wofs, wsrc=wsrc:
                               g.dma_start(
                        out=wsl[:, wofs:wofs + DC * 512].rearrange(
                            "p (c n) -> p c n", c=DC),
                        in_=wsrc.rearrange("(c p) n -> p c n", p=128)),
                        deps=[st["wsl_r"][sl]], dma=True)

                tw = [qkv_load(wk), qkv_load(wv)]

                with contextlib.ExitStack() as sc_ph:
                    h1s = sc_ph.enter_context(
                        nc.sbuf_tensor(f"h1s_{tag}", [128, 2 * D], BF16))
                    t_h1s_r = [None, None]
                    for j in range(NT):
                        xj = x_sb[:, j * D:(j + 1) * D]
                        ab = (j % 2) * 64
                        hb = (j % 2) * D
                        dsq = opn("scalar", lambda s, xj=xj, ab=ab, hb=hb:
                                  s.activation(
                                      out=h1s[:, hb:hb + D], in_=xj,
                                      func=AF.Square,
                                      accum_out=stats[:, ab:ab + 1]),
                                  deps=[t_x, res["x"][j], t_h1s_r[j % 2]])
                        dsm = opn("vector", lambda v, xj=xj, ab=ab: v.tensor_reduce(
                            out=sca(ab, 1), in_=xj, axis=AX.X, op=ALU.add, negate=True),
                            deps=[t_x, res["x"][j], t_h1s_r[j % 2]])
                        dv1 = opn("vector", lambda v, ab=ab: v.tensor_scalar(
                            out=sca(ab, 2), in0=sca(ab, 1),
                            scalar1=1.0 / D, scalar2=None, op0=ALU.mult), deps=[dsm])
                        dv2 = opn("vector", lambda v, ab=ab: v.tensor_tensor(
                            out=sca(ab, 6), in0=sca(ab, 2),
                            in1=sca(ab, 2), op=ALU.mult), deps=[dv1])
                        dvar = opn("vector", lambda v, ab=ab: v.scalar_tensor_tensor(
                            out=sca(ab, 3), in0=sca(ab, 0), scalar=1.0 / D,
                            in1=sca(ab, 6), op0=ALU.mult, op1=ALU.subtract),
                            deps=[dsq, dv2])
                        dr1a = opn("scalar", lambda s, ab=ab: s.activation(
                            out=sca(ab, 4), in_=sca(ab, 3), func=AF.Ln,
                            bias=cst[:, 0:1], scale=1.0), deps=[dvar, t_init0])
                        dr1 = opn("scalar", lambda s, ab=ab: s.activation(
                            out=sca(ab, 5), in_=sca(ab, 4), func=AF.Exp,
                            scale=cst[:, 2:3]), deps=[dr1a])
                        dms1 = opn("vector", lambda v, ab=ab: v.tensor_tensor(
                            out=sca(ab, 7), in0=sca(ab, 5),
                            in1=sca(ab, 5), op=ALU.mult), deps=[dr1])
                        dms = opn("vector", lambda v, ab=ab: v.tensor_tensor(
                            out=sca(ab, 8), in0=sca(ab, 3),
                            in1=sca(ab, 7), op=ALU.mult), deps=[dms1])
                        dr2a = opn("scalar", lambda s, ab=ab: s.activation(
                            out=sca(ab, 9), in_=sca(ab, 8), func=AF.Ln,
                            bias=cst[:, 1:2], scale=1.0), deps=[dms])
                        dr2 = opn("scalar", lambda s, ab=ab: s.activation(
                            out=sca(ab, 10), in_=sca(ab, 9),
                            func=AF.Exp, scale=cst[:, 2:3]), deps=[dr2a])
                        dh = opn("vector", lambda v, j=j, xj=xj, ab=ab:
                                 v.tensor_scalar(
                                     out=hid[:, j * D:(j + 1) * D], in0=xj,
                                     scalar1=sca(ab, 2), scalar2=sca(ab, 5),
                                     op0=ALU.add, op1=ALU.mult),
                                 deps=[dr1, t_hid[j]])
                        t_hid[j] = dh
                        dx1 = opn("vector", lambda v, ab=ab: (
                            v.tensor_tensor(out=sca(ab, 11), in0=sca(ab, 2),
                                            in1=sca(ab, 5), op=ALU.mult),
                            v.tensor_scalar(out=sca(ab, 12), in0=sca(ab, 5),
                                            scalar1=0.5, scalar2=1.0, op0=ALU.mult,
                                            op1=ALU.add))[-1], deps=[dr1])
                        dx2 = opn("vector", lambda v, ab=ab: v.tensor_scalar(
                            out=sca(ab, 13), in0=sca(ab, 11),
                            scalar1=0.5, scalar2=None, op0=ALU.mult), deps=[dx1])
                        dxu = opn("vector", lambda v, xj=xj, ab=ab: v.tensor_scalar(
                            out=xj, in0=xj, scalar1=sca(ab, 12),
                            scalar2=sca(ab, 13), op0=ALU.mult,
                            op1=ALU.add), deps=[dx2, dh, dsq])
                        res["x"][j] = dxu
                        dh1 = opn("scalar", lambda s, j=j, ab=ab, hb=hb:
                                  s.activation(
                                      out=h1s[:, hb:hb + D],
                                      in_=hid[:, j * D:(j + 1) * D], func=AF.Copy,
                                      scale=sca(ab, 10)), deps=[dr2, dh])
                        for g in range(4):
                            dcs = list(range(g * 4, g * 4 + 4))

                            def ev(v, pt, dcs=dcs, j=j):
                                for i, dc in enumerate(dcs):
                                    m = v.tensor_scalar(
                                        out=xT[:, dc * TC + j * 128:
                                               dc * TC + (j + 1) * 128],
                                        in0=pt[:, i * 128:(i + 1) * 128],
                                        scalar1=nwa_col[:, dc:dc + 1], scalar2=None,
                                        op0=ALU.mult)
                                return m
                            dev = tpose_group(
                                [h1s[:, hb + dc * 128:hb + (dc + 1) * 128]
                                 for dc in dcs],
                                ev, [dh1, res.get("xT_r")])
                            for dc in dcs:
                                t_xT[dc * NT + j] = dev
                            t_h1s_r[j % 2] = dev
                    res["xT_r"] = None
                    sch.flush()
                if upto == "a1":
                    sc_a2.close()
                    return

                t_store_q = []
                t_store_kv = []
                with sc_a2:

                    def qkv_nblock(kind, t_w, wofs, nbc, shard_rows):
                        last_mm = None
                        for j in range(NT):
                            bank = j % 4
                            s2 = j % 4
                            r2 = j % 2
                            rb0 = r2 * 512
                            sb_ = 16 + s2 * 12
                            qs = qst[:, r2 * 512:(r2 + 1) * 512]

                            def mm(t, j=j, bank=bank, wofs=wofs):
                                for dc in range(DC):
                                    m = t.matmul(pb[bank][:],
                                                 xT[:, dc * TC + j * 128: dc * TC + (j + 1) * 128],
                                                 wsl[:, wofs + dc * 512:
                                                     wofs + (dc + 1) * 512],
                                                 start=(dc == 0),
                                                 stop=(dc == DC - 1))
                                return m
                            dmm = opn("tensor", mm,
                                      deps=[t_w, res["pb"][bank]] +
                                           [t_xT[dc * NT + j] for dc in range(DC)])
                            last_mm = dmm
                            if kind == "v":
                                dev = opn("vector", lambda v, bank=bank, rb0=rb0:
                                          v.tensor_copy(
                                              out=ropo[:, rb0:rb0 + 512],
                                              in_=pb[bank][:]),
                                          deps=[dmm, st["ropo_r"][r2]])
                                res["pb"][bank] = dev

                                def ev(v, pt, qs=qs):
                                    for hh in range(4):
                                        m = v.tensor_copy(
                                            out=qs[:, hh * 128:(hh + 1) * 128],
                                            in_=pt[:, hh * 128:(hh + 1) * 128])
                                    return m
                                dev2 = tpose_group(
                                    [ropo[:, rb0 + hh * 128:rb0 + (hh + 1) * 128]
                                     for hh in range(4)],
                                    ev, [dev] + st["qst_r"][r2])
                                st["ropo_r"][r2] = dev2
                                toks = []
                                for hh in range(4):
                                    toks.append(opn("sync", lambda s, hh=hh, j=j,
                                                    qs=qs: s.dma_start(
                                        out=kv_in[512 + hh * 128: 512 + (hh + 1) * 128,
                                                  j * 128:(j + 1) * 128],
                                        in_=qs[:, hh * 128:(hh + 1) * 128]),
                                        deps=[dev2, res.get("kvin_w")], dma=True))
                                    st["qst_r"][r2][hh] = toks[-1]
                                t_store_kv.extend(toks)
                                continue
                            dsq = opn("scalar", lambda s, bank=bank,
                                      sb_=sb_: (
                                [s.activation(out=qdmp[:, hh * 128:(hh + 1) * 128],
                                              in_=pb[bank][:, hh * 128:(hh + 1) * 128],
                                              func=AF.Square,
                                              accum_out=stats[:, sb_ + hh:
                                                              sb_ + hh + 1])
                                 for hh in range(4)])[-1],
                                deps=[dmm, st["dump_r"], st["stats"][s2]])
                            st["dump_r"] = dsq
                            drsa = opn("scalar", lambda s, sb_=sb_: s.activation(
                                out=stats[:, sb_ + 4:sb_ + 8],
                                in_=stats[:, sb_:sb_ + 4],
                                func=AF.Ln, bias=cst[:, 1:2], scale=1.0 / HD),
                                deps=[dsq])
                            drs = opn("scalar", lambda s, sb_=sb_: s.activation(
                                out=stats[:, sb_ + 8:sb_ + 12],
                                in_=stats[:, sb_ + 4:sb_ + 8],
                                func=AF.Exp, scale=cst[:, 2:3]), deps=[drsa])
                            dnm = opn("vector", lambda v, bank=bank, nbc=nbc,
                                      sb_=sb_: (
                                [v.scalar_tensor_tensor(
                                    out=ropeb[:, hh * 128:(hh + 1) * 128],
                                    in0=pb[bank][:, hh * 128:(hh + 1) * 128],
                                    scalar=stats[:, sb_ + 8 + hh:sb_ + 9 + hh],
                                    in1=nbc,
                                    op0=ALU.mult, op1=ALU.mult) for hh in range(4)])[-1],
                                deps=[drs, t_nbc, st["ropeb_w"]])
                            st["stats"][s2] = dnm
                            res["pb"][bank] = dnm

                            def rope1f(v, j=j):
                                cs = r1c[:, j * 64:(j + 1) * 64]
                                sn = r1s[:, j * 64:(j + 1) * 64]
                                for hh in range(4):
                                    b1 = ropeb[:, hh * 128: hh * 128 + 64]
                                    b2 = ropeb[:, hh * 128 + 64: hh * 128 + 128]
                                    sl = slice(hh * 64, hh * 64 + 64)
                                    v.tensor_tensor(out=tmpa[:, sl], in0=b1, in1=cs,
                                                    op=ALU.mult)
                                    v.tensor_tensor(out=tmpb[:, sl], in0=b2, in1=sn,
                                                    op=ALU.mult)
                                    v.tensor_tensor(out=tmpc[:, sl], in0=b2, in1=cs,
                                                    op=ALU.mult)
                                    m = v.tensor_tensor(out=tmpd[:, sl], in0=b1, in1=sn,
                                                        op=ALU.mult)
                                return m
                            drpa = opn("vector", rope1f,
                                       deps=[dnm, t_r1, st["tmp_r"]])

                            def rope2f(v, rb0=rb0):
                                for hh in range(4):
                                    o1 = ropo[:, rb0 + hh * 128:
                                              rb0 + hh * 128 + 64]
                                    o2 = ropo[:, rb0 + hh * 128 + 64:
                                              rb0 + hh * 128 + 128]
                                    sl = slice(hh * 64, hh * 64 + 64)
                                    v.tensor_tensor(out=o1, in0=tmpa[:, sl],
                                                    in1=tmpb[:, sl], op=ALU.subtract)
                                    m = v.tensor_tensor(out=o2, in0=tmpc[:, sl],
                                                        in1=tmpd[:, sl], op=ALU.add)
                                return m
                            drp = opn("vector", rope2f,
                                      deps=[drpa, st["ropo_r"][r2]])
                            st["tmp_r"] = drp
                            st["ropeb_w"] = drp

                            def ev(v, pt, qs=qs):
                                for hh in range(4):
                                    m = v.tensor_copy(
                                        out=qs[:, hh * 128:(hh + 1) * 128],
                                        in_=pt[:, hh * 128:(hh + 1) * 128])
                                return m
                            dev = tpose_group(
                                [ropo[:, rb0 + hh * 128:rb0 + (hh + 1) * 128]
                                 for hh in range(4)],
                                ev, [drp] + st["qst_r"][r2])
                            st["ropo_r"][r2] = dev
                            dst_tgt = kv_in if kind == "k" else q_in
                            dst_list = t_store_kv if kind == "k" else t_store_q
                            dst_war = "kvin_w" if kind == "k" else "qin_w"
                            for hh in range(4):
                                toks = []
                                for row in shard_rows[hh]:
                                    toks.append(opn("sync", lambda s, hh=hh, row=row,
                                                    j=j, dst_tgt=dst_tgt, qs=qs:
                                                    s.dma_start(
                                        out=dst_tgt[row: row + 128,
                                                    j * 128:(j + 1) * 128],
                                        in_=qs[:, hh * 128:(hh + 1) * 128]),
                                        deps=[dev, res.get(dst_war)], dma=True))
                                st["qst_r"][r2][hh] = toks[-1]
                                dst_list.extend(toks)
                        return last_mm

                    st["wsl_r"][0] = qkv_nblock("k", tw[0], 0, nwk_bc[:],
                                                [[g * 128] for g in range(4)])
                    tw.append(qkv_load(wq[0]))
                    st["wsl_r"][1] = qkv_nblock("v", tw[1], DC * 512, None, None)
                    tw.append(qkv_load(wq[1]))
                    t_cc2 = opn("gpsimd", lambda g: g.collective_compute(
                        "AllGather", ALU.bypass, replica_groups=RG,
                        ins=[kv_in.ap().opt()], outs=[kv_out.ap().opt()]),
                        deps=t_store_kv + [res.get("kvout_r")])
                    res["kvin_w"] = t_cc2
                    for nb in range(4):
                        heads = [4 * nb + hh for hh in range(4)]
                        st["wsl_r"][nb % 2] = qkv_nblock(
                            "q", tw[2 + nb], (nb % 2) * DC * 512, nwq_bc[:],
                            [[h * 128] for h in heads])
                        if nb < 2:
                            tw.append(qkv_load(wq[nb + 2]))
                    sch.flush()

                t_cc1 = opn("gpsimd", lambda g: g.collective_compute(
                    "AllGather", ALU.bypass, replica_groups=RG,
                    ins=[q_in.ap().opt()], outs=[q_out.ap().opt()]),
                    deps=t_store_q + [res.get("qout_r")])
                res["qin_w"] = t_cc1
                if upto == "a2":
                    sch.flush()
                    return

                sch.barrier()
                at_toks = []
                with contextlib.ExitStack() as sc_ph:
                    sbp = lambda nm, shp, dt: sc_ph.enter_context(
                        nc.sbuf_tensor(f"{nm}_{tag}", shp, dt))
                    qTh = sbp("qTh", [128, S], BF16)
                    kTh = sbp("kTh", [128, S], BF16)
                    vstag = sbp("vstag", [128, 512], BF16)
                    v_sb = sbp("v_sb", [128, S], BF16)
                    pts = sbp("pts", [128, 2 * 512], BF16)
                    smsk = sbp("smsk", [128, 3 * 512], BF16)
                    sqs = sbp("sqs", [128, 64], F32)
                    esq = sbp("esq", [128, 64], F32)
                    msb = sbp("msb", [128, 4], BF16)
                    nmsbf = sbp("nmsbf", [128, 4], F32)
                    srec = sbp("srec", [128, 4], F32)
                    srecb = sbp("srecb", [128, 4], BF16)
                    tmpq = sbp("tmpq", [128, 52], F32)
                    rnb = sbp("rnb", [128, 512], F32)
                    atst = sbp("atst", [128, 512], BF16)
                    mka = sbp("mka", [128, 4 * 512], BF16)
                    mkt = sbp("mkt", [128, 4 * 512], BF16)
                    t_mk = opn("gpsimd", lambda g: (
                        [g.dma_start(out=mka[:, tt * 512:(tt + 1) * 512],
                                     in_=maska[tt]) for tt in range(4)] +
                        [g.dma_start(out=mkt[:, tt * 512:(tt + 1) * 512],
                                     in_=masktA[tt]) for tt in range(4)]),
                        dma=True, ndma=8)
                    ast = {"qkh_r": None, "atst_r": None, "smsk_r": None,
                           "nmrow_r": [None, None], "t_pts": [None, None],
                           "t_smsk2": [None, None], "sqs_r": None,
                           "esq_r": None, "msb_r": None, "t_qkh": None,
                           "sq_r": None, "srecb_r": None, "srow_r0": None,
                           "srow_r1": None, "last_av": None, "dump_r": None}
                    sbanks = (pb[2], pb[3], pmx)

                    def sb_get(i):
                        return res["pmx"] if i == 2 else res["pb"][2 + i]

                    def sb_set(i, tok):
                        if i == 2:
                            res["pmx"] = tok
                        else:
                            res["pb"][2 + i] = tok

                    def emit_pass1_gen(qc):
                        p1_smsk_r = ast["smsk_r"]
                        for tt in range(4):
                            for kc in range(qc + 1):
                                n = tt * (qc + 1) + kc
                                bank = n % 2
                                qlo = qc * 512 + tt * 128
                                col = slice(tt * 16 + kc, tt * 16 + kc + 1)

                                def mm(t, qlo=qlo, kc=kc, bank=bank):
                                    return t.matmul(pb[bank][:],
                                                    qTh[:, qlo:qlo + 128],
                                                    kTh[:, kc * 512:(kc + 1) * 512],
                                                    start=True, stop=True)
                                dmm = opn("tensor", mm,
                                          deps=[ast["t_qkh"], res["pb"][bank]])
                                if kc == qc:
                                    dms = opn("vector", lambda v, bank=bank, tt=tt:
                                              v.scalar_tensor_tensor(
                                                  out=smsk[:, 1024:1536],
                                                  in0=pb[bank][:], scalar=1.0,
                                                  in1=mka[:, tt * 512:(tt + 1) * 512],
                                                  op0=ALU.mult, op1=ALU.add),
                                              deps=[dmm, t_mk, p1_smsk_r])
                                    drd = opn("vector", lambda v, col=col:
                                              v.tensor_reduce(
                                                  out=sqs[:, col],
                                                  in_=smsk[:, 1024:1536], axis=AX.X,
                                                  op=ALU.max, negate=True),
                                              deps=[dms, ast["sqs_r"]])
                                    dsx = opn("scalar", lambda s, col=col:
                                              s.activation(
                                                  out=vstag[:, 0:512],
                                                  in_=smsk[:, 1024:1536],
                                                  func=AF.Exp, bias=sqs[:, col],
                                                  scale=1.0,
                                                  accum_out=esq[:, col]),
                                              deps=[drd, ast["esq_r"],
                                                    ast["dump_r"], t_vsb])
                                    p1_smsk_r = dsx
                                    res["pb"][bank] = dms
                                else:
                                    drd = opn("vector", lambda v, col=col,
                                              bank=bank: v.tensor_reduce(
                                                  out=sqs[:, col],
                                                  in_=pb[bank][:], axis=AX.X,
                                                  op=ALU.max, negate=True),
                                              deps=[dmm, ast["sqs_r"]])
                                    dsx = opn("scalar", lambda s, col=col,
                                              bank=bank: s.activation(
                                                  out=vstag[:, 0:512],
                                                  in_=pb[bank][:],
                                                  func=AF.Exp, bias=sqs[:, col],
                                                  scale=1.0,
                                                  accum_out=esq[:, col]),
                                              deps=[drd, ast["esq_r"],
                                                    ast["dump_r"], t_vsb])
                                    res["pb"][bank] = dsx
                                ast["dump_r"] = dsx
                                yield
                        ast["smsk_r"] = p1_smsk_r

                    def emit_mrow(qc):
                        nc_ = qc + 1
                        ns = qc % 2
                        dmn = opn("vector", lambda v, nc_=nc_: (
                            [v.tensor_reduce(out=msb[:, tt:tt + 1],
                                             in_=sqs[:, tt * 16: tt * 16 + nc_],
                                             axis=AX.X, op=ALU.min)
                             for tt in range(4)])[-1],
                            deps=[ast["msb_r"],
                                  Tok("vector", sch.cnt["vector"]),
                                  Tok("scalar", sch.cnt["scalar"])])
                        dmf = opn("vector", lambda v: v.tensor_copy(
                            out=nmsbf[:], in_=msb[:]), deps=[dmn])
                        dS = None
                        prev_dte = None
                        for tt in range(4):
                            dts = opn("vector", lambda v, tt=tt, nc_=nc_:
                                      v.tensor_scalar(
                                          out=tmpq[:, 0:nc_],
                                          in0=sqs[:, tt * 16: tt * 16 + nc_],
                                          scalar1=nmsbf[:, tt:tt + 1],
                                          scalar2=-1.0, op0=ALU.subtract,
                                          op1=ALU.mult),
                                      deps=[dmf, prev_dte])
                            dte = opn("scalar", lambda s, nc_=nc_: s.activation(
                                out=tmpq[:, 16:16 + nc_], in_=tmpq[:, 0:nc_],
                                func=AF.Exp), deps=[dts])
                            prev_dte = dte
                            dtm = opn("vector", lambda v, tt=tt, nc_=nc_:
                                      v.tensor_tensor(
                                          out=tmpq[:, 32:32 + nc_],
                                          in0=tmpq[:, 16:16 + nc_],
                                          in1=esq[:, tt * 16: tt * 16 + nc_],
                                          op=ALU.mult), deps=[dte])
                            dsm = opn("vector", lambda v, nc_=nc_: v.tensor_reduce(
                                out=tmpq[:, 48:49], in_=tmpq[:, 32:32 + nc_],
                                axis=AX.X, op=ALU.add), deps=[dtm])
                            dS = opn("vector", lambda v, tt=tt: v.reciprocal(
                                out=srec[:, tt:tt + 1], in_=tmpq[:, 48:49]),
                                deps=[dsm, ast["sq_r"]])
                        ast["sqs_r"] = dS
                        ast["esq_r"] = dS
                        bank = tb_state["bank"]
                        tb_state["bank"] ^= 1
                        pt = ptr_ab[bank]
                        dtr = opn("tensor", lambda t, pt=pt: (
                            [t.transpose(pt[0:1, tt * 128:(tt + 1) * 128],
                                         msb[:, tt:tt + 1], eye_b[:])
                             for tt in range(4)])[-1],
                            deps=[dmn, res["ptrg"][bank]])
                        ast["msb_r"] = dtr
                        dnm = opn("vector", lambda v, pt=pt, ns=ns: v.tensor_copy(
                            out=nmrow[0:1, ns * 512:(ns + 1) * 512],
                            in_=pt[0:1, 0:512]),
                            deps=[dtr, ast["nmrow_r"][ns]])
                        res["ptrg"][bank] = dnm
                        dcp = opn("vector", lambda v: v.tensor_copy(
                            out=srecb[:], in_=srec[:]),
                            deps=[dS, ast["srecb_r"]])
                        ast["sq_r"] = dcp
                        bank2 = tb_state["bank"]
                        tb_state["bank"] ^= 1
                        pt2 = ptr_ab[bank2]
                        dtr2 = opn("tensor", lambda t, pt2=pt2: (
                            [t.transpose(pt2[0:1, tt * 128:(tt + 1) * 128],
                                         srecb[:, tt:tt + 1], eye_b[:])
                             for tt in range(4)])[-1],
                            deps=[dcp, res["ptrg"][bank2]])
                        ast["srecb_r"] = dtr2
                        dsr = opn("vector", lambda v, pt2=pt2, ns=ns: v.tensor_copy(
                            out=rowb[0:1, ns * 512:(ns + 1) * 512],
                            in_=pt2[0:1, 0:512]),
                            deps=[dtr2, ast[f"srow_r{ns}"], res.get("rowb_r")])
                        res["ptrg"][bank2] = dsr
                        res["rowb_r"] = dsr
                        return dnm, dsr

                    def emit_pass2_gen(qc, dnm, dsr, hl):
                        nkc = (qc + 1) * 4
                        ns = qc % 2
                        nmr = nmrow[0:1, ns * 512:(ns + 1) * 512]
                        pend = None

                        def em_av(kc, dex, nkc=nkc):
                            def av(t, kc=kc, nkc=nkc):
                                return t.matmul(
                                    pb[4][:], v_sb[:, kc * 128:(kc + 1) * 128],
                                    pts[:, (kc % 2) * 512:(kc % 2 + 1) * 512],
                                    start=(kc == 0), stop=(kc == nkc - 1))
                            davk = opn("tensor", av,
                                       deps=[dex, t_vsb] +
                                            ([res["pb"][4]] if kc == 0 else []))
                            ast["t_pts"][kc % 2] = davk
                            return davk

                        for kc in range(nkc):
                            bi = kc % 3
                            bank = sbanks[bi]

                            def mm(t, kc=kc, bank=bank, qc=qc, nmr=nmr):
                                t.matmul(bank[:],
                                         kTh[:, kc * 128:(kc + 1) * 128],
                                         qTh[:, qc * 512:(qc + 1) * 512],
                                         start=True, stop=False)
                                return t.matmul(bank[:], ones_r[:], nmr,
                                                start=False, stop=True)
                            dmm = opn("tensor", mm, deps=[dnm, sb_get(bi)])
                            ast["nmrow_r"][ns] = dmm
                            if kc >= qc * 4:
                                kl = kc - qc * 4
                                dmk = opn("vector", lambda v, bank=bank, kl=kl,
                                          kc=kc: v.scalar_tensor_tensor(
                                              out=smsk[:, (kc % 2) * 512:
                                                      (kc % 2 + 1) * 512],
                                              in0=bank[:], scalar=1.0,
                                              in1=mkt[:, kl * 512:(kl + 1) * 512],
                                              op0=ALU.mult, op1=ALU.add),
                                          deps=[dmm, t_mk, ast["t_smsk2"][kc % 2]])
                                dex = opn("scalar", lambda s, kc=kc: s.activation(
                                    out=pts[:, (kc % 2) * 512:(kc % 2 + 1) * 512],
                                    in_=smsk[:, (kc % 2) * 512:(kc % 2 + 1) * 512],
                                    func=AF.Exp), deps=[dmk, ast["t_pts"][kc % 2]])
                                ast["t_smsk2"][kc % 2] = dex
                                sb_set(bi, dmk)
                            else:
                                dex = opn("scalar", lambda s, bank=bank, kc=kc:
                                          s.activation(
                                              out=pts[:, (kc % 2) * 512:
                                                      (kc % 2 + 1) * 512],
                                              in_=bank[:], func=AF.Exp),
                                          deps=[dmm, ast["t_pts"][kc % 2]])
                                sb_set(bi, dex)
                            if pend is not None:
                                em_av(*pend)
                            pend = (kc, dex)
                            yield
                        last_av = em_av(*pend)
                        dbc = opn("tensor", lambda t, ns=ns: t.matmul(
                            pb[3][:], ones_rf[:],
                            rowb[0:1, ns * 512:(ns + 1) * 512],
                            start=True, stop=True), deps=[dsr, res["pb"][3]])
                        ast[f"srow_r{ns}"] = dbc
                        res["rowb_r"] = dbc
                        dnorm0 = opn("vector", lambda v: v.tensor_copy(
                            out=rnb[:], in_=pb[3][:]), deps=[dbc])
                        dnorm = opn("vector", lambda v: v.tensor_tensor(
                            out=atst[:], in0=pb[4][:], in1=rnb[:], op=ALU.mult),
                            deps=[dnorm0, last_av, ast["atst_r"]])
                        res["pb"][3] = dnorm
                        res["pb"][4] = dnorm
                        dst = opn("sync", lambda s, qc=qc, hl=hl: s.dma_start(
                            out=at_in[(qc // 2) * 256 + hl * 128:
                                      (qc // 2) * 256 + (hl + 1) * 128,
                                      (qc % 2) * 512:(qc % 2) * 512 + 512],
                            in_=atst[:]), deps=[dnorm, res.get("atin_w")], dma=True)
                        ast["atst_r"] = dst
                        at_toks.append(dst)
                        ast["last_av"] = last_av
                        yield

                    def lk(g):
                        return [g.indirect_dma_start(
                            out=kTh[:, i * TC:(i + 1) * TC], out_offset=None,
                            in_=kv_out.ap(),
                            in_offset=bass.IndirectOffsetOnAxis(
                                ap=idx_sb[:, 16 + i:17 + i], axis=0))
                            for i in range(NCORE)]
                    t_kh = opn("gpsimd", lk, deps=[t_cc1, t_idx], dma=True, ndma=8)
                    t_vsb = None
                    vslot_r = [None, None]
                    for i in range(NCORE):
                        for g2 in range(2):
                            vs = (i * 2 + g2) % 2
                            vbuf = vstag[:, 0:512] if vs == 0 \
                                else smsk[:, 1024:1536]
                            t_vg = opn("gpsimd", lambda g, i=i, g2=g2, vbuf=vbuf:
                                       g.indirect_dma_start(
                                out=vbuf, out_offset=None,
                                in_=kv_out.ap(),
                                in_offset=bass.IndirectOffsetOnAxis(
                                    ap=idx_sb[:, 24 + i:25 + i], axis=0),
                                element_offset=g2 * 512),
                                deps=[t_cc2, t_idx, vslot_r[vs]], dma=True)
                            bs = [g2 * 4 + bi for bi in range(4)]

                            def ev(v, pt, i=i, bs=bs):
                                for ii, b in enumerate(bs):
                                    m = v.tensor_copy(
                                        out=v_sb[:, (i * 8 + b) * 128:
                                                 (i * 8 + b + 1) * 128],
                                        in_=pt[:, ii * 128:(ii + 1) * 128])
                                return m
                            t_vsb = vslot_r[vs] = tpose_group(
                                [vbuf[:, bi * 128:(bi + 1) * 128]
                                 for bi in range(4)],
                                ev, [t_vg])

                    ast["smsk_r"] = t_vsb
                    for hl in range(2):
                        def lq(g, hl=hl):
                            return [g.indirect_dma_start(
                                out=qTh[:, i * TC:(i + 1) * TC], out_offset=None,
                                in_=q_out.ap(),
                                in_offset=bass.IndirectOffsetOnAxis(
                                    ap=idx_sb[:, hl * 8 + i:hl * 8 + i + 1], axis=0))
                                for i in range(NCORE)]
                        ast["t_qkh"] = opn("gpsimd", lq,
                                           deps=[t_cc1, t_idx, ast["qkh_r"], t_kh],
                                           dma=True, ndma=8)
                        prev = None
                        for qc in range(16):
                            g1 = emit_pass1_gen(qc)
                            g2 = (emit_pass2_gen(prev[0], prev[1], prev[2], hl)
                                  if prev is not None else None)
                            a1, a2 = True, g2 is not None
                            while a1 or a2:
                                if a1:
                                    try:
                                        next(g1)
                                    except StopIteration:
                                        a1 = False
                                if a2:
                                    try:
                                        next(g2)
                                    except StopIteration:
                                        a2 = False
                            dnm_q, dsr_q = emit_mrow(qc)
                            prev = (qc, dnm_q, dsr_q)
                        for _ in emit_pass2_gen(prev[0], prev[1], prev[2], hl):
                            pass

                        ast["qkh_r"] = ast["last_av"]
                    res["qout_r"] = ast["qkh_r"]
                    res["kvout_r"] = ast["qkh_r"]
                    sch.flush()
                if upto == "attn":
                    return

                t_cc3 = opn("gpsimd", lambda g: g.collective_compute(
                    "AllGather", ALU.bypass, replica_groups=RG,
                    ins=[at_in.ap().opt()], outs=[at_out.ap().opt()]),
                    deps=at_toks + [res.get("atout_r")])
                res["atin_w"] = t_cc3
                sch.barrier()
                with contextlib.ExitStack() as sc_ph:
                    sbp = lambda nm, shp, dt: sc_ph.enter_context(
                        nc.sbuf_tensor(f"{nm}_{tag}", shp, dt))
                    attg = sbp("attg", [128, DC * TC], BF16)
                    wso = sbp("wso", [128, DC * 512], BF16)

                    def la(g):
                        return [g.indirect_dma_start(
                            out=attg[:, ic * TC:(ic + 1) * TC], out_offset=None,
                            in_=at_out.ap(),
                            in_offset=bass.IndirectOffsetOnAxis(
                                ap=idx_sb[:, 32 + ic:33 + ic], axis=0))
                            for ic in range(DC)]
                    t_ag = opn("gpsimd", la, deps=[t_cc3, t_idx], dma=True, ndma=16)
                    res["atout_r"] = t_ag
                    t_wso_r = None
                    for nb in range(4):
                        t_wo = opn("gpsimd", lambda g, nb=nb: g.dma_start(
                            out=wso[:].rearrange("p (c n) -> p c n", c=DC),
                            in_=wo[nb].rearrange("(c p) n -> p c n", p=128)),
                            deps=[t_wso_r], dma=True)
                        for j in range(NT):
                            bank = j % 4

                            def mm(t, j=j, bank=bank):
                                for ic in range(DC):
                                    m = t.matmul(pb[bank][:],
                                                 attg[:, ic * TC + j * 128:
                                                      ic * TC + (j + 1) * 128],
                                                 wso[:, ic * 512:(ic + 1) * 512],
                                                 start=(ic == 0), stop=(ic == DC - 1))
                                return m
                            dmm = opn("tensor", mm, deps=[t_wo, t_ag, res["pb"][bank]])
                            t_wso_r = dmm
                            dup = opn("vector", lambda v, j=j, nb=nb, bank=bank: (
                                v.tensor_tensor(
                                    out=hid[:, j * D + nb * 512: j * D + (nb + 1) * 512],
                                    in0=hid[:, j * D + nb * 512: j * D + (nb + 1) * 512],
                                    in1=pb[bank][:], op=ALU.add),
                                v.scalar_tensor_tensor(
                                    out=x_sb[:, j * D + nb * 512: j * D + (nb + 1) * 512],
                                    in0=pb[bank][:], scalar=0.5,
                                    in1=x_sb[:, j * D + nb * 512: j * D + (nb + 1) * 512],
                                    op0=ALU.mult, op1=ALU.add))[-1],
                                deps=[dmm, t_hid[j], res["x"][j]])
                            t_hid[j] = dup
                            res["x"][j] = dup
                            res["pb"][bank] = dup
                    sch.flush()
                if upto == "oproj":
                    return

                hid_stack.close_after_experts = True
                sch.barrier()
                with contextlib.ExitStack() as sc_ph:
                    h2s = sc_ph.enter_context(
                        nc.sbuf_tensor(f"h2s_{tag}", [128, D], BF16))
                    t_h2s_r = None
                    for j in range(NT):
                        dsq = opn("scalar", lambda s, j=j: s.activation(
                            out=h2s[:], in_=hid[:, j * D:(j + 1) * D], func=AF.Square,
                            accum_out=stats[:, 0:1]),
                            deps=[t_hid[j], t_h2s_r, res["stats"]])
                        drsa = opn("scalar", lambda s: s.activation(
                            out=stats[:, 1:2], in_=stats[:, 0:1], func=AF.Ln,
                            bias=cst[:, 1:2], scale=1.0 / D), deps=[dsq])
                        drs = opn("scalar", lambda s: s.activation(
                            out=stats[:, 2:3], in_=stats[:, 1:2],
                            func=AF.Exp, scale=cst[:, 2:3]), deps=[drsa])
                        res["stats"] = drs
                        dh2 = opn("scalar", lambda s, j=j: s.activation(
                            out=h2s[:], in_=hid[:, j * D:(j + 1) * D], func=AF.Copy,
                            scale=stats[:, 2:3]), deps=[drs])
                        for g in range(4):
                            dcs = list(range(g * 4, g * 4 + 4))

                            def ev(v, pt, dcs=dcs, j=j):
                                for i, dc in enumerate(dcs):
                                    m = v.tensor_scalar(
                                        out=xT[:, dc * TC + j * 128:
                                               dc * TC + (j + 1) * 128],
                                        in0=pt[:, i * 128:(i + 1) * 128],
                                        scalar1=nwf_col[:, dc:dc + 1], scalar2=None,
                                        op0=ALU.mult)
                                return m
                            dev = tpose_group(
                                [h2s[:, dc * 128:(dc + 1) * 128] for dc in dcs],
                                ev, [dh2] + [t_xT[dc * NT + j] for dc in dcs])
                            for dc in dcs:
                                t_xT[dc * NT + j] = dev
                            t_h2s_r = dev
                    sch.flush()

                hid_stack.close()
                sch.barrier()
                with contextlib.ExitStack() as sc_ph:
                    sbp = lambda nm, shp, dt: sc_ph.enter_context(
                        nc.sbuf_tensor(f"{nm}_{tag}", shp, dt))
                    xeT = sbp("xeT", [128, NI * TC], BF16)
                    wga = sbp("wga", [128, DC * 512], BF16)
                    wgb_s = sbp("wgb2", [128, DC * 512], BF16)
                    wda = sbp("wda", [128, NI * 512], BF16)
                    wdb_s = sbp("wdb2", [128, NI * 512], BF16)
                    comb = sbp("comb", [128, NT * E], F32)
                    est = {"wsl": [None, None], "dsl": [None, None],
                           "xeT_w": [[None] * 2 for _ in range(NI)],
                           "xeT_r": None, "xeT_last": None, "comb": [None] * NT,
                           "xT_last": None}
                    for j in range(NT):
                        bank = j % 4

                        def mm(t, j=j, bank=bank):
                            for dc in range(DC):
                                m = t.matmul(pb[bank][:, 0:E],
                                             xT[:, dc * TC + j * 128: dc * TC + (j + 1) * 128],
                                             gw_sb[:, dc * E:(dc + 1) * E],
                                             start=(dc == 0), stop=(dc == DC - 1))
                            return m
                        dmm = opn("tensor", mm,
                                  deps=[t_gw, res["pb"][bank]] +
                                       [t_xT[dc * NT + j] for dc in range(DC)])
                        dcp0 = opn("vector", lambda v, bank=bank: v.tensor_copy(
                            out=stats[:, 40:48], in_=pb[bank][:, 0:E]),
                            deps=[dmm, res["stats"]])
                        dcp = opn("vector", lambda v: v.tensor_reduce(
                            out=stats[:, 32:33], in_=stats[:, 40:48],
                            axis=AX.X, op=ALU.max, negate=True), deps=[dcp0])
                        res["pb"][bank] = dcp0
                        cj = comb[:, j * E:(j + 1) * E]
                        dex = opn("scalar", lambda s, cj=cj: s.activation(
                            out=cj, in_=stats[:, 40:48],
                            func=AF.Exp, bias=stats[:, 32:33], scale=1.0), deps=[dcp])
                        g1 = opn("vector", lambda v, cj=cj: v.tensor_reduce(
                            out=stats[:, 33:34], in_=cj, axis=AX.X, op=ALU.add),
                            deps=[dex])
                        g2 = opn("vector", lambda v: v.reciprocal(
                            out=stats[:, 34:35], in_=stats[:, 33:34]), deps=[g1])
                        g3 = opn("vector", lambda v, cj=cj: v.tensor_scalar(
                            out=cj, in0=cj, scalar1=stats[:, 34:35],
                            scalar2=None, op0=ALU.mult), deps=[g2])
                        g4 = opn("vector", lambda v, cj=cj: v.tensor_reduce(
                            out=stats[:, 35:36], in_=cj, axis=AX.X, op=ALU.max),
                            deps=[g3])
                        g5 = opn("vector", lambda v, cj=cj: v.tensor_scalar(
                            out=stats[:, 40:48], in0=cj, scalar1=stats[:, 35:36],
                            scalar2=None, op0=ALU.is_ge), deps=[g4])
                        g6 = opn("vector", lambda v, cj=cj: v.tensor_tensor(
                            out=stats[:, 40:48], in0=cj, in1=stats[:, 40:48],
                            op=ALU.subtract), deps=[g5])
                        g7 = opn("vector", lambda v: v.tensor_reduce(
                            out=stats[:, 36:37], in_=stats[:, 40:48],
                            axis=AX.X, op=ALU.max), deps=[g6])
                        g8 = opn("vector", lambda v: (
                            v.tensor_tensor(out=stats[:, 37:38], in0=stats[:, 35:36],
                                            in1=stats[:, 36:37], op=ALU.add)),
                            deps=[g7])
                        g9 = opn("vector", lambda v: v.tensor_scalar(
                            out=stats[:, 37:38], in0=stats[:, 37:38],
                            scalar1=1e-9, scalar2=None, op0=ALU.add), deps=[g8])
                        g10 = opn("vector", lambda v: v.reciprocal(
                            out=stats[:, 38:39], in_=stats[:, 37:38]), deps=[g9])
                        g11 = opn("vector", lambda v: v.tensor_scalar(
                            out=stats[:, 39:40], in0=stats[:, 38:39],
                            scalar1=0.5, scalar2=None, op0=ALU.mult), deps=[g10])
                        g12 = opn("vector", lambda v, cj=cj: v.tensor_scalar(
                            out=stats[:, 40:48], in0=cj, scalar1=stats[:, 36:37],
                            scalar2=None, op0=ALU.is_ge), deps=[g7, g11])
                        g13 = opn("vector", lambda v, cj=cj: v.tensor_tensor(
                            out=cj, in0=cj, in1=stats[:, 40:48], op=ALU.mult),
                            deps=[g12])
                        dt2 = opn("vector", lambda v, cj=cj: v.tensor_scalar(
                            out=cj, in0=cj, scalar1=stats[:, 39:40],
                            scalar2=None, op0=ALU.mult), deps=[g13])
                        res["stats"] = dt2
                        est["comb"][j] = dt2
                    for e in range(E):
                        for phase_i, wsrc in enumerate([wgb, wub]):
                            for gi in range(3):
                                sl_i = gi % 2
                                slot = (wga, wgb_s)[sl_i]
                                t_wl = opn("gpsimd", lambda g, e=e, gi=gi, wsrc=wsrc,
                                           slot=slot: g.dma_start(
                                    out=slot[:].rearrange("p (c n) -> p c n", c=DC),
                                    in_=wsrc[e, gi].rearrange("(c p) n -> p c n", p=128)),
                                    deps=[est["wsl"][sl_i]], dma=True)
                                ni_g = (4, 4, 3)[gi]
                                last_mm = None
                                for il in range(ni_g):
                                    ig = gi * 4 + il
                                    for tc2 in range(2):
                                        bank = (il * 2 + tc2) % 4

                                        def mm(t, il=il, tc2=tc2, bank=bank, slot=slot):
                                            for dc in range(DC):
                                                m = t.matmul(
                                                    pb[bank][:],
                                                    slot[:, dc * 512 + il * 128:
                                                         dc * 512 + (il + 1) * 128],
                                                    xT[:, dc * TC + tc2 * 512:
                                                       dc * TC + (tc2 + 1) * 512],
                                                    start=(dc == 0), stop=(dc == DC - 1))
                                            return m
                                        dmm = opn("tensor", mm,
                                                  deps=[t_wl, res["pb"][bank]])
                                        est["xT_last"] = dmm
                                        last_mm = dmm
                                        sl = xeT[:, ig * TC + tc2 * 512:
                                                 ig * TC + (tc2 + 1) * 512]
                                        if phase_i == 0:
                                            dev = opn("scalar", lambda s, sl=sl,
                                                      bank=bank: s.activation(
                                                out=sl, in_=pb[bank][:], func=AF.Silu),
                                                deps=[dmm, est["xeT_r"]])
                                        else:
                                            dev = opn("vector", lambda v, sl=sl,
                                                      bank=bank: v.tensor_tensor(
                                                out=sl, in0=sl, in1=pb[bank][:],
                                                op=ALU.mult),
                                                deps=[dmm, est["xeT_w"][ig][tc2]])
                                        est["xeT_w"][ig][tc2] = dev
                                        est["xeT_last"] = dev
                                        res["pb"][bank] = dev
                                est["wsl"][sl_i] = last_mm
                        for dcb in range(4):
                            sl_i = dcb % 2
                            dslot = (wda, wdb_s)[sl_i]
                            t_wl = opn("gpsimd", lambda g, e=e, dcb=dcb, dslot=dslot:
                                       g.dma_start(
                                out=dslot[:].rearrange("p (c n) -> p c n", c=NI),
                                in_=wdb[e, dcb].rearrange("(c p) n -> p c n", p=128)),
                                deps=[est["dsl"][sl_i]], dma=True)
                            last_mm = None
                            xeT_deps = [est["xeT_w"][ig][tc2] for ig in range(NI)
                                        for tc2 in range(2)]
                            for j in range(NT):
                                bank = j % 4

                                def mm(t, j=j, bank=bank, dslot=dslot):
                                    for il in range(NI):
                                        m = t.matmul(pb[bank][:],
                                                     xeT[:, il * TC + j * 128:
                                                         il * TC + (j + 1) * 128],
                                                     dslot[:, il * 512:(il + 1) * 512],
                                                     start=(il == 0), stop=(il == NI - 1))
                                    return m
                                dmm = opn("tensor", mm,
                                          deps=[t_wl, res["pb"][bank]] + xeT_deps)
                                last_mm = dmm
                                est["xeT_r"] = dmm
                                dup = opn("vector", lambda v, j=j, dcb=dcb, e=e,
                                          bank=bank: v.scalar_tensor_tensor(
                                    out=x_sb[:, j * D + dcb * 512: j * D + (dcb + 1) * 512],
                                    in0=pb[bank][:],
                                    scalar=comb[:, j * E + e: j * E + e + 1],
                                    in1=x_sb[:, j * D + dcb * 512: j * D + (dcb + 1) * 512],
                                    op0=ALU.mult, op1=ALU.add),
                                    deps=[dmm, res["x"][j], est["comb"][j]])
                                res["x"][j] = dup
                                res["pb"][bank] = dup
                            est["dsl"][sl_i] = last_mm
                    res["xT_r"] = est["xT_last"]
                    sch.flush()

        def attn_block():
            sch.barrier()
            with contextlib.ExitStack() as sc_blk:
                sbb = lambda nm, shp, dt: sc_blk.enter_context(
                    nc.sbuf_tensor(f"{nm}_s2", shp, dt))
                hT = sbb("hT", [128, DC * TC], BF16)
                qs = sbb("qs", [128, NT * D], BF16)
                scr1 = sbb("scr1", [128, 512], F32)
                scr2 = sbb("scr2", [128, 512], F32)
                scr3 = sbb("scr3", [128, 512], F32)
                scr4 = sbb("scr4", [128, 512], F32)
                bbc = sbb("bbc", [128, 512], F32)
                r2cs = sbb("r2cs", [128, 512], F32)
                r2ss = sbb("r2ss", [128, 512], F32)
                rK = sbb("rK", [128, NT], F32)
                t_hT = [None] * (NT * DC)
                s2 = {"bbc_r": None, "scr_r": None, "r2_r": None, "hT_r": None,
                      "qs": {}, "kst": {}, "rk": [None] * NT}

                with contextlib.ExitStack() as sc_ph:
                    hs = sc_ph.enter_context(nc.sbuf_tensor("hs_s2", [128, D], BF16))
                    t_hs_r = None
                    for j in range(NT):
                        xj = x_sb[:, j * D:(j + 1) * D]
                        dsq = opn("scalar", lambda s, xj=xj: s.activation(
                            out=hs[:], in_=xj, func=AF.Square, accum_out=stats[:, 0:1]),
                            deps=[res["x"][j], t_hs_r, res["stats"]])
                        dsm = opn("vector", lambda v, xj=xj: v.tensor_reduce(
                            out=stats[:, 1:2], in_=xj, axis=AX.X, op=ALU.add,
                            negate=True), deps=[res["x"][j], res["stats"]])
                        dv1 = opn("vector", lambda v: v.tensor_scalar(
                            out=stats[:, 2:3], in0=stats[:, 1:2],
                            scalar1=1.0 / D, scalar2=None, op0=ALU.mult), deps=[dsm])
                        dv1b = opn("vector", lambda v: v.tensor_tensor(
                            out=stats[:, 6:7], in0=stats[:, 2:3],
                            in1=stats[:, 2:3], op=ALU.mult), deps=[dv1])
                        dvar = opn("vector", lambda v: v.scalar_tensor_tensor(
                            out=stats[:, 3:4], in0=stats[:, 0:1], scalar=1.0 / D,
                            in1=stats[:, 6:7], op0=ALU.mult, op1=ALU.subtract),
                            deps=[dsq, dv1b])
                        dr1a = opn("scalar", lambda s: s.activation(
                            out=stats[:, 4:5], in_=stats[:, 3:4], func=AF.Ln,
                            bias=cst[:, 0:1], scale=1.0), deps=[dvar])
                        dr1 = opn("scalar", lambda s: s.activation(
                            out=stats[:, 5:6], in_=stats[:, 4:5],
                            func=AF.Exp, scale=cst[:, 2:3]), deps=[dr1a])
                        dv2a = opn("vector", lambda v: v.tensor_tensor(
                            out=stats[:, 6:7], in0=stats[:, 5:6],
                            in1=stats[:, 5:6], op=ALU.mult), deps=[dr1])
                        dv2 = opn("vector", lambda v: v.tensor_tensor(
                            out=stats[:, 7:8], in0=stats[:, 3:4],
                            in1=stats[:, 6:7], op=ALU.mult), deps=[dv2a])
                        dr2a = opn("scalar", lambda s: s.activation(
                            out=stats[:, 8:9], in_=stats[:, 7:8], func=AF.Ln,
                            bias=cst[:, 0:1], scale=1.0), deps=[dv2])
                        dr2 = opn("scalar", lambda s: s.activation(
                            out=stats[:, 9:10], in_=stats[:, 8:9],
                            func=AF.Exp, scale=cst[:, 2:3]), deps=[dr2a])
                        dhs0 = opn("vector", lambda v: v.tensor_tensor(
                            out=stats[:, 10:11], in0=stats[:, 5:6],
                            in1=stats[:, 9:10], op=ALU.mult), deps=[dr2])
                        dhs = opn("vector", lambda v, xj=xj: v.tensor_scalar(
                            out=hs[:], in0=xj, scalar1=stats[:, 2:3],
                            scalar2=stats[:, 10:11], op0=ALU.add,
                            op1=ALU.mult), deps=[dhs0, dsq])
                        dxa = opn("vector", lambda v: (
                            v.tensor_scalar(out=stats[:, 11:12], in0=stats[:, 5:6],
                                            scalar1=1.0, scalar2=None, op0=ALU.add),
                            v.tensor_tensor(out=stats[:, 12:13], in0=stats[:, 2:3],
                                            in1=stats[:, 5:6], op=ALU.mult))[-1],
                            deps=[dr1])
                        dxu = opn("vector", lambda v, xj=xj: v.tensor_scalar(
                            out=xj, in0=xj, scalar1=stats[:, 11:12],
                            scalar2=stats[:, 12:13], op0=ALU.mult,
                            op1=ALU.add), deps=[dxa, dhs])
                        res["x"][j] = dxu
                        res["stats"] = dxu
                        for g in range(4):
                            dcs = list(range(g * 4, g * 4 + 4))

                            def ev(v, pt, dcs=dcs, j=j):
                                for i, dc in enumerate(dcs):
                                    m = v.tensor_scalar(
                                        out=hT[:, dc * TC + j * 128:
                                               dc * TC + (j + 1) * 128],
                                        in0=pt[:, i * 128:(i + 1) * 128],
                                        scalar1=lnw_col[:, dc:dc + 1],
                                        scalar2=lnb_col[:, dc:dc + 1],
                                        op0=ALU.mult, op1=ALU.add)
                                return m
                            dev = tpose_group(
                                [hs[:, dc * 128:(dc + 1) * 128] for dc in dcs],
                                ev, [dhs])
                            for dc in dcs:
                                t_hT[dc * NT + j] = dev
                            t_hs_r = dev
                    sch.flush()

                def proj_pass(nb, wsrc_ap, bias_ap, consume, wsl, wsl_key):
                    t_w = opn("gpsimd", lambda g: g.dma_start(
                        out=wsl[:].rearrange("p (c n) -> p c n", c=DC),
                        in_=wsrc_ap.rearrange("(c p) n -> p c n", p=128)),
                        deps=[s2.get(wsl_key)], dma=True)
                    t_b = opn("sync", lambda s: s.dma_start(
                        out=rowb[0:1, 0:512], in_=bias_ap),
                        deps=[res.get("rowb_r")], dma=True)
                    t_bb = opn("tensor", lambda t: t.matmul(
                        pb[4][:], ones_rf[:], rowb[0:1, 0:512], start=True, stop=True),
                        deps=[t_b, res["pb"][4]])
                    res["rowb_r"] = t_bb
                    t_bbc = opn("vector", lambda v: v.tensor_copy(
                        out=bbc[:], in_=pb[4][:]), deps=[t_bb, s2["bbc_r"]])
                    res["pb"][4] = t_bbc
                    last = None
                    for j in range(NT):
                        bank = j % 3

                        def mm(t, j=j, bank=bank):
                            for dc in range(DC):
                                m = t.matmul(pb[bank][:],
                                             hT[:, dc * TC + j * 128: dc * TC + (j + 1) * 128],
                                             wsl[:, dc * 512:(dc + 1) * 512],
                                             start=(dc == 0), stop=(dc == DC - 1))
                            return m
                        dmm = opn("tensor", mm,
                                  deps=[t_w, res["pb"][bank]] +
                                       [t_hT[dc * NT + j] for dc in range(DC)])
                        last = dmm
                        s2["hT_r"] = dmm
                        consume(j, bank, dmm, t_bbc)
                    s2[wsl_key] = last

                with contextlib.ExitStack() as sc_ph:
                    sbp = lambda nm, shp, dt: sc_ph.enter_context(
                        nc.sbuf_tensor(f"{nm}_s2", shp, dt))
                    kst = sbp("kst", [128, NT * D], BF16)
                    wsl2 = sbp("wsl2", [128, DC * 512], BF16)
                    for pr in range(4):
                        def ck(j, bank, dmm, t_bbc, pr=pr):
                            dev = opn("vector", lambda v, j=j, bank=bank, pr=pr:
                                      v.scalar_tensor_tensor(
                                          out=kst[:, j * D + pr * 512:
                                                  j * D + (pr + 1) * 512],
                                          in0=pb[bank][:], scalar=1.0, in1=bbc[:],
                                          op0=ALU.mult, op1=ALU.add),
                                      deps=[dmm, t_bbc])
                            res["pb"][bank] = dev
                            s2["bbc_r"] = dev
                            s2["kst"][(j, pr)] = dev
                        s2.setdefault("bbc_r", None)
                        proj_pass(4 + pr, win[4 + pr], bin_[4 + pr], ck, wsl2, "wsl2_r")
                    for j in range(NT):
                        t_r2 = opn("sync", lambda s, j=j: [
                            s.dma_start(out=r2cs[:], in_=rope2c[j * 128:(j + 1) * 128, :]),
                            s.dma_start(out=r2ss[:],
                                        in_=rope2s[j * 128:(j + 1) * 128, :])],
                            deps=[s2["r2_r"]], dma=True, ndma=2)
                        k1 = kst[:, j * D: j * D + 512]
                        k2 = kst[:, j * D + 512: j * D + 1024]

                        def ropeA(v, k1=k1, k2=k2):
                            v.tensor_tensor(out=scr1[:], in0=k1, in1=r2cs[:], op=ALU.mult)
                            v.tensor_tensor(out=scr2[:], in0=k2, in1=r2ss[:], op=ALU.mult)
                            v.tensor_tensor(out=scr3[:], in0=k2, in1=r2cs[:], op=ALU.mult)
                            return v.tensor_tensor(out=scr4[:], in0=k1, in1=r2ss[:],
                                                   op=ALU.mult)
                        drpa = opn("vector", ropeA,
                                   deps=[t_r2, s2["kst"][(j, 0)], s2["kst"][(j, 1)],
                                         s2["scr_r"]])

                        def ropeB(v, k1=k1, k2=k2):
                            v.tensor_tensor(out=k1, in0=scr1[:], in1=scr2[:],
                                            op=ALU.subtract)
                            return v.tensor_tensor(out=k2, in0=scr3[:], in1=scr4[:],
                                                   op=ALU.add)
                        drp = opn("vector", ropeB, deps=[drpa])
                        s2["r2_r"] = drp
                        s2["scr_r"] = drp

                        def ksq(s, j=j):
                            scrs = [scr1, scr2, scr3, scr4]
                            for pr in range(4):
                                m = s.activation(out=scrs[pr][:],
                                                 in_=kst[:, j * D + pr * 512:
                                                        j * D + (pr + 1) * 512],
                                                 func=AF.Square,
                                                 accum_out=stats[:, 16 + pr:17 + pr])
                            return m
                        dq = opn("scalar", ksq,
                                 deps=[drp, s2["kst"][(j, 2)], s2["kst"][(j, 3)],
                                       res["stats"]])
                        s2["scr_r"] = dq
                        dsumv = opn("vector", lambda v: v.tensor_reduce(
                            out=stats[:, 20:21], in_=stats[:, 16:20], axis=AX.X,
                            op=ALU.add), deps=[dq])
                        drka = opn("scalar", lambda s: s.activation(
                            out=stats[:, 21:22], in_=stats[:, 20:21],
                            func=AF.Ln), deps=[dsumv])
                        drk = opn("scalar", lambda s, j=j: s.activation(
                            out=rK[:, j:j + 1], in_=stats[:, 21:22],
                            func=AF.Exp, scale=cst[:, 2:3]), deps=[drka])
                        res["stats"] = drk
                        s2["rk"][j] = drk
                    dst_a = []
                    for pr in range(4):
                        a_last = [None]

                        def cv(j, bank, dmm, t_bbc, pr=pr, a_last=a_last):
                            dv = opn("vector", lambda v, bank=bank:
                                     v.scalar_tensor_tensor(
                                         out=scr2[:], in0=pb[bank][:], scalar=1.0,
                                         in1=bbc[:], op0=ALU.mult, op1=ALU.add),
                                     deps=[dmm, t_bbc, s2["scr_r"]])
                            res["pb"][bank] = dv
                            s2["bbc_r"] = dv
                            dkv = opn("vector", lambda v, j=j, pr=pr:
                                      v.scalar_tensor_tensor(
                                          out=scr3[:],
                                          in0=kst[:, j * D + pr * 512:
                                                  j * D + (pr + 1) * 512],
                                          scalar=rK[:, j:j + 1], in1=scr2[:],
                                          op0=ALU.mult, op1=ALU.mult),
                                      deps=[dv, s2["rk"][j]])
                            dA = opn("tensor", lambda t, j=j: t.matmul(
                                pmx[0:1, 0:512], ones_cf[:], scr3[:],
                                start=(j == 0), stop=(j == NT - 1)),
                                deps=[dkv] + ([res["pmx"]] if j == 0 else []))
                            s2["scr_r"] = dA
                            a_last[0] = dA
                        proj_pass(8 + pr, win[8 + pr], bin_[8 + pr], cv, wsl2, "wsl2_r")
                        dcp = opn("vector", lambda v: v.tensor_copy(
                            out=rowb[0:1, 512:1024], in_=pmx[0:1, 0:512]),
                            deps=[a_last[0], res.get("rowb_r")])
                        res["pmx"] = dcp
                        dsa = opn("sync", lambda s, pr=pr: s.dma_start(
                            out=a_in[0:1, pr * 512:(pr + 1) * 512],
                            in_=rowb[0:1, 512:1024]), deps=[dcp], dma=True)
                        res["rowb_r"] = dsa
                        dst_a.append(dsa)
                    t_ccA = opn("gpsimd", lambda g: g.collective_compute(
                        "AllReduce", ALU.add, replica_groups=RG,
                        ins=[a_in.ap().opt()], outs=[a_out.ap().opt()]), deps=dst_a)
                    sch.flush()

                sch.barrier()
                with contextlib.ExitStack() as sc_ph:
                    sbp = lambda nm, shp, dt: sc_ph.enter_context(
                        nc.sbuf_tensor(f"{nm}_s2", shp, dt))
                    wslq = sbp("wslq", [128, DC * 512], BF16)
                    abc = sbp("abc", [128, D], F32)
                    for nb in range(4):
                        def cq(j, bank, dmm, t_bbc, nb=nb):
                            dev = opn("vector", lambda v, j=j, bank=bank, nb=nb:
                                      v.scalar_tensor_tensor(
                                          out=qs[:, j * D + nb * 512:
                                                 j * D + (nb + 1) * 512],
                                          in0=pb[bank][:], scalar=1.0, in1=bbc[:],
                                          op0=ALU.mult, op1=ALU.add),
                                      deps=[dmm, t_bbc])
                            res["pb"][bank] = dev
                            s2["bbc_r"] = dev
                            s2["qs"][(j, nb)] = dev
                        proj_pass(nb, win[nb], bin_[nb], cq, wslq, "wslq_r")
                    for j in range(NT):
                        t_r2 = opn("sync", lambda s, j=j: [
                            s.dma_start(out=r2cs[:], in_=rope2c[j * 128:(j + 1) * 128, :]),
                            s.dma_start(out=r2ss[:],
                                        in_=rope2s[j * 128:(j + 1) * 128, :])],
                            deps=[s2["r2_r"]], dma=True, ndma=2)
                        q1 = qs[:, j * D: j * D + 512]
                        q2 = qs[:, j * D + 512: j * D + 1024]

                        def ropeA(v, q1=q1, q2=q2):
                            v.tensor_tensor(out=scr1[:], in0=q1, in1=r2cs[:], op=ALU.mult)
                            v.tensor_tensor(out=scr2[:], in0=q2, in1=r2ss[:], op=ALU.mult)
                            v.tensor_tensor(out=scr3[:], in0=q2, in1=r2cs[:], op=ALU.mult)
                            return v.tensor_tensor(out=scr4[:], in0=q1, in1=r2ss[:],
                                                   op=ALU.mult)
                        drpa = opn("vector", ropeA,
                                   deps=[t_r2, s2["qs"][(j, 0)], s2["qs"][(j, 1)],
                                         s2["scr_r"]])

                        def ropeB(v, q1=q1, q2=q2):
                            v.tensor_tensor(out=q1, in0=scr1[:], in1=scr2[:],
                                            op=ALU.subtract)
                            return v.tensor_tensor(out=q2, in0=scr3[:], in1=scr4[:],
                                                   op=ALU.add)
                        drp = opn("vector", ropeB, deps=[drpa])
                        s2["r2_r"] = drp
                        s2["scr_r"] = drp
                        s2["qs"][(j, "rope")] = drp
                    t_al = opn("sync", lambda s: s.dma_start(
                        out=rowb[0:1, 0:1024], in_=a_out[0:1, 0:1024]),
                        deps=[t_ccA, res.get("rowb_r")], dma=True)
                    dabc = None
                    for hh in range(2):
                        tbx = opn("tensor", lambda t, hh=hh: t.matmul(
                            pb[4][:], ones_rf[:], rowb[0:1, hh * 512:(hh + 1) * 512],
                            start=True, stop=True), deps=[t_al, res["pb"][4]])
                        dabc = opn("vector", lambda v, hh=hh: v.tensor_copy(
                            out=abc[:, hh * 512:(hh + 1) * 512], in_=pb[4][:]),
                            deps=[tbx])
                        res["pb"][4] = dabc
                    t_al2 = opn("sync", lambda s: s.dma_start(
                        out=rowb[0:1, 0:1024], in_=a_out[0:1, 1024:2048]),
                        deps=[dabc, t_ccA], dma=True)
                    res["rowb_r"] = t_al2
                    for hh in range(2):
                        tbx = opn("tensor", lambda t, hh=hh: t.matmul(
                            pb[4][:], ones_rf[:], rowb[0:1, hh * 512:(hh + 1) * 512],
                            start=True, stop=True), deps=[t_al2, res["pb"][4]])
                        dabc = opn("vector", lambda v, hh=hh: v.tensor_copy(
                            out=abc[:, 1024 + hh * 512: 1024 + (hh + 1) * 512],
                            in_=pb[4][:]), deps=[tbx])
                        res["pb"][4] = dabc
                    res["rowb_r"] = dabc
                    t_YT = [None] * (NT * DC)
                    with contextlib.ExitStack() as sc_y:
                        ys = sc_y.enter_context(
                            nc.sbuf_tensor("ys_s2", [128, D], BF16))
                        t_ys_r = None
                        for j in range(NT):
                            dY = opn("vector", lambda v, j=j: v.tensor_tensor(
                                out=ys[:], in0=qs[:, j * D:(j + 1) * D], in1=abc[:],
                                op=ALU.mult),
                                deps=[s2["qs"][(j, "rope")], s2["qs"][(j, 2)],
                                      s2["qs"][(j, 3)], dabc, t_ys_r])
                            for g in range(4):
                                dcs = list(range(g * 4, g * 4 + 4))

                                def ev(v, pt, dcs=dcs, j=j):
                                    for i, dc in enumerate(dcs):
                                        m = v.tensor_copy(
                                            out=hT[:, dc * TC + j * 128:
                                                   dc * TC + (j + 1) * 128],
                                            in_=pt[:, i * 128:(i + 1) * 128])
                                    return m
                                dev = tpose_group(
                                    [ys[:, dc * 128:(dc + 1) * 128] for dc in dcs],
                                    ev, [dY, s2["hT_r"]])
                                for dc in dcs:
                                    t_YT[dc * NT + j] = dev
                                t_ys_r = dev
                        sch.flush()
                    for nb in range(4):
                        t_w = opn("gpsimd", lambda g, nb=nb: g.dma_start(
                            out=wslq[:].rearrange("p (c n) -> p c n", c=DC),
                            in_=wout[nb].rearrange("(c p) n -> p c n", p=128)),
                            deps=[s2.get("wslq_r")], dma=True)
                        t_b = opn("sync", lambda s, nb=nb: s.dma_start(
                            out=rowb[0:1, 0:512], in_=bout[nb]),
                            deps=[res.get("rowb_r")], dma=True)
                        t_bb = opn("tensor", lambda t: t.matmul(
                            pb[4][:], ones_rf[:], rowb[0:1, 0:512], start=True,
                            stop=True), deps=[t_b, res["pb"][4]])
                        res["rowb_r"] = t_bb
                        t_bbc = opn("vector", lambda v: v.tensor_copy(
                            out=bbc[:], in_=pb[4][:]), deps=[t_bb, s2["bbc_r"]])
                        res["pb"][4] = t_bbc
                        last = None
                        for j in range(NT):
                            bank = j % 3

                            def mm(t, j=j, bank=bank):
                                for dc in range(DC):
                                    m = t.matmul(pb[bank][:],
                                                 hT[:, dc * TC + j * 128:
                                                    dc * TC + (j + 1) * 128],
                                                 wslq[:, dc * 512:(dc + 1) * 512],
                                                 start=(dc == 0), stop=(dc == DC - 1))
                                return m
                            dmm = opn("tensor", mm,
                                      deps=[t_w, res["pb"][bank]] +
                                           [t_YT[dc * NT + j] for dc in range(DC)])
                            last = dmm
                            dup0 = opn("vector", lambda v, bank=bank: v.tensor_tensor(
                                out=scr1[:], in0=pb[bank][:], in1=bbc[:], op=ALU.add),
                                deps=[dmm, t_bbc, s2["scr_r"]])
                            dup = opn("vector", lambda v, j=j, nb=nb: v.tensor_tensor(
                                out=x_sb[:, j * D + nb * 512: j * D + (nb + 1) * 512],
                                in0=x_sb[:, j * D + nb * 512: j * D + (nb + 1) * 512],
                                in1=scr1[:], op=ALU.add),
                                deps=[dup0, res["x"][j]])
                            res["x"][j] = dup
                            s2["scr_r"] = dup
                            s2["bbc_r"] = dup
                            res["pb"][bank] = dup
                        s2["wslq_r"] = last
                    sch.flush()

        ffn_block("s1")
        if upto not in ("a1", "a2", "attn", "oproj", "s1"):
            attn_block()
            if upto != "s2":
                ffn_block("s3")

        t_st = opn("sync", lambda s: s.dma_start(
            out=y_out.rearrange("(t p) d -> p t d", p=128),
            in_=x_sb[:].rearrange("p (t d) -> p t d", t=NT)),
            deps=[res["x"][j] for j in range(NT)], dma=True)
        lasts = sch.last()
        opn("sync", lambda s: s.dma_start(out=y_out[0:1, 0:1], in_=x_sb[0:1, 0:1]),
            deps=[t_st] + lasts, dma=True)
        finals = [(k, sch.cnt[k]) for k in kernel_dma_keys_sync()]
        sch.flush()
        sch.block.sync(lambda h: [h.wait_ge(sch.sem[k], v) for k, v in finals])

    return nc



_CACHE = {}


def _prep(inputs):
    f32 = np.float32
    X = np.asarray(inputs["L0_attn_X"], f32).reshape(S, D)

    qw = np.asarray(inputs["L0_ffn_q_w"], f32)
    kw = np.asarray(inputs["L0_ffn_k_w"], f32)
    vw = np.asarray(inputs["L0_ffn_v_w"], f32)
    ow = np.asarray(inputs["L0_ffn_o_w"], f32)
    wq_h = np.ascontiguousarray(qw.T.reshape(D, 4, 512).transpose(1, 0, 2))
    wk_h = np.ascontiguousarray(kw.T)
    wv_h = np.ascontiguousarray(vw.T)
    wo_h = np.ascontiguousarray(ow.T.reshape(D, 4, 512).transpose(1, 0, 2))
    wgate_h = np.ascontiguousarray(np.asarray(inputs["L0_ffn_gate_w"], f32).T)
    wg = np.asarray(inputs["L0_ffn_wg"], f32)
    wu = np.asarray(inputs["L0_ffn_wu"], f32)
    wd = np.asarray(inputs["L0_ffn_wd"], f32)
    wgb_h = np.zeros((E, 3, D, 512), f32)
    wub_h = np.zeros((E, 3, D, 512), f32)
    for e in range(E):
        wgt = wg[e].T
        wut = wu[e].T
        for gi, (lo, hi) in enumerate([(0, 512), (512, 1024), (1024, 1408)]):
            wgb_h[e, gi, :, :hi - lo] = wgt[:, lo:hi]
            wub_h[e, gi, :, :hi - lo] = wut[:, lo:hi]
    wdb_h = np.zeros((E, 4, IFF, 512), f32)
    for e in range(E):
        wdt = wd[e].T
        for dcb in range(4):
            wdb_h[e, dcb] = wdt[:, dcb * 512:(dcb + 1) * 512]

    perm = np.concatenate([np.arange(0, 1024, 2), np.arange(1, 1024, 2),
                           np.arange(1024, 2048)])
    Win = np.asarray(inputs["L0_attn_W_in"], f32)
    bin0 = np.asarray(inputs["L0_attn_b_in"], f32)
    Winp = np.concatenate([Win[:, 0:2048][:, perm], Win[:, 2048:4096][:, perm],
                           Win[:, 4096:6144][:, perm]], axis=1)
    binp = np.concatenate([bin0[0:2048][perm], bin0[2048:4096][perm],
                           bin0[4096:6144][perm]])
    win_h = np.ascontiguousarray(Winp.reshape(D, 12, 512).transpose(1, 0, 2))
    bin_h = np.ascontiguousarray(binp.reshape(12, 1, 512))
    Woutp = np.asarray(inputs["L0_attn_W_out"], f32)[perm, :]
    wout_h = np.ascontiguousarray(Woutp.reshape(D, 4, 512).transpose(1, 0, 2))
    bout_h = np.ascontiguousarray(np.asarray(inputs["L0_attn_b_out"], f32)
                                  .reshape(4, 1, 512))

    qq = np.arange(128)[:, None]
    maska_h = np.zeros((4, 128, 512), f32)
    masktA_h = np.zeros((4, 128, 512), f32)
    kk = np.arange(512)[None, :]
    for t in range(4):
        maska_h[t] = np.where(kk <= (t * 128 + qq), 0.0, -1e30)
    for kl in range(4):
        kp = np.arange(128)[:, None]
        qcol = np.arange(512)[None, :]
        allowed = (kl * 128 + kp) <= ((qcol // 128) * 128 + (qcol % 128))
        masktA_h[kl] = np.where(allowed, 0.0, -1e30)

    pos_all = np.arange(S, dtype=np.float64)
    f1 = np.power(1e6, -np.arange(0, HD, 2, dtype=np.float64) / HD)
    ang1 = pos_all[:, None] * f1[None, :]
    r1c_full = np.cos(ang1).astype(f32)
    r1s_full = np.sin(ang1).astype(f32)
    f2 = np.power(1e4, -np.arange(0, 1024, 2, dtype=np.float64) / 1024)
    ang2 = pos_all[:, None] * f2[None, :]
    r2c_full = np.cos(ang2).astype(f32)
    r2s_full = np.sin(ang2).astype(f32)

    shared = dict(
        wq=wq_h, wk=wk_h, wv=wv_h, wo=wo_h, wgate=wgate_h,
        wgb=wgb_h, wub=wub_h, wdb=wdb_h,
        nw_attn=np.ascontiguousarray(
            np.asarray(inputs["L0_ffn_attn_norm_w"], f32).reshape(DC, 128).T),
        nw_ffn=np.ascontiguousarray(
            np.asarray(inputs["L0_ffn_ffn_norm_w"], f32).reshape(DC, 128).T),
        nw_q=np.asarray(inputs["L0_ffn_q_norm_w"], f32).reshape(1, HD),
        nw_k=np.asarray(inputs["L0_ffn_k_norm_w"], f32).reshape(1, HD),
        win=win_h, bin=bin_h, wout=wout_h, bout=bout_h,
        lnw=np.ascontiguousarray(
            np.asarray(inputs["L0_attn_ln_w"], f32).reshape(DC, 128).T),
        lnb=np.ascontiguousarray(
            np.asarray(inputs["L0_attn_ln_b"], f32).reshape(DC, 128).T),
        maska=maska_h, masktA=masktA_h,
    )
    in_maps = []
    pp = np.arange(128, dtype=np.uint32)
    for c in range(NCORE):
        m = dict(shared)
        idx = np.zeros((48, 128), np.uint32)
        g = c // 2
        for i in range(NCORE):
            idx[0 + i] = i * 2048 + (2 * c) * 128 + pp
            idx[8 + i] = i * 2048 + (2 * c + 1) * 128 + pp
            idx[16 + i] = i * 1024 + g * 128 + pp
            idx[24 + i] = i * 1024 + 512 + g * 128 + pp
        for r in range(NCORE):
            for hl in range(2):
                idx[32 + r * 2 + hl] = r * 2048 + c * 256 + hl * 128 + pp
        m["idxs"] = np.ascontiguousarray(idx.T)
        m["x_in"] = np.ascontiguousarray(X[c * TC:(c + 1) * TC])
        m["rope1c"] = np.ascontiguousarray(r1c_full[c * TC:(c + 1) * TC])
        m["rope1s"] = np.ascontiguousarray(r1s_full[c * TC:(c + 1) * TC])
        m["rope2c"] = np.ascontiguousarray(r2c_full[c * TC:(c + 1) * TC])
        m["rope2s"] = np.ascontiguousarray(r2s_full[c * TC:(c + 1) * TC])
        in_maps.append(m)
    return in_maps


def kernel(**inputs):
    upto = inputs.pop("_upto", "all")
    if upto not in _CACHE:
        _CACHE[upto] = build_program(upto)
    nc = _CACHE[upto]
    in_maps = _prep(inputs)
    res = run_bass_kernel_spmd(nc, in_maps, core_ids=list(range(NCORE)))
    out = np.concatenate([res.results[c]["y_out"] for c in range(NCORE)], axis=0)
    return out.reshape(1, S, D).astype(np.float32)
